# version 10
# baseline (speedup 1.0000x reference)
import contextlib
import numpy as np
import concourse.bass as bass
import concourse.mybir as mybir
from concourse.alu_op_type import AluOpType as OP
from concourse.bass_utils import run_bass_kernel_spmd

F32 = mybir.dt.float32
BF16 = mybir.dt.bfloat16
AF = mybir.ActivationFunctionType
AX = mybir.AxisListType.X

D = 1024
TC = 256
NTT = TC // 128
NKT = D // 128
DST = 16
DTR = 64
DCONV = 8
HID = 4096
HE = 512
NE = 8
EPS = 1e-5
BIG = 1e30

ENGINES = ("sync", "tensor", "vector", "scalar", "gpsimd")


class Prog:
    def __init__(self):
        self.ops = {e: [] for e in ENGINES}
        self.cnt = {}
        for e in ENGINES:
            self.cnt[e] = 0

    def bump(self, name):
        if name not in self.cnt:
            self.cnt[name] = 0

    def _mk(self, eng, fn, after, semname, amt):
        self.bump(semname)
        waits = {}
        for tok in after:
            if tok is None:
                continue
            s, v = tok
            if s == eng:
                continue
            if waits.get(s, 0) < v:
                waits[s] = v
        self.cnt[semname] += amt
        self.ops[eng].append((waits, fn, semname, amt))
        return (semname, self.cnt[semname])


class Buf:
    __slots__ = ("w", "r")

    def __init__(self):
        self.w = None
        self.r = []


def ex(prog, eng, fn, reads=(), writes=(), extra=(), q=None):
    after = list(extra)
    for b in reads:
        after.append(b.w)
    for b in writes:
        after.append(b.w)
        after.extend(b.r)
    tok = prog._mk(eng, fn, after, q if q else eng, 16 if q else 1)
    for b in reads:
        b.r.append(tok)
    for b in writes:
        b.w = tok
        b.r = []
    return tok


class Rot:

    def __init__(self):
        self.p = 0
        self.held = set()

    def take(self):
        for _ in range(12):
            s = self.p % 6
            self.p += 1
            if s not in self.held:
                return s
        raise RuntimeError("no free slot")


def build_nc(T=4096, debug=False):
    NCH = T // TC
    nc = bass.Bass("TRN2", target_bir_lowering=False)
    dram = {}

    def din(name, shape):
        dram[name] = nc.dram_tensor(name, list(shape), F32, kind="ExternalInput").ap()
        return dram[name]

    x_h = din("x", (T, D))
    w_inproj = din("L0_attn_in_proj_w", (2 * D, D))
    w_conv = din("L0_attn_conv_w", (D, DCONV))
    b_conv = din("L0_attn_conv_b", (D,))
    w_xproj = din("L0_attn_x_proj_w", (DTR + 2 * DST, D))
    w_dtproj = din("L0_attn_dt_proj_w", (D, DTR))
    b_dt = din("L0_attn_dt_proj_b", (D,))
    a_log = din("L0_attn_A_log", (D, DST))
    d_param = din("L0_attn_D_param", (D,))
    w_outproj = din("L0_attn_out_proj_w", (D, D))
    g_ln1 = din("L0_attn_ln1_g", (D,))
    b_ln1 = din("L0_attn_ln1_b", (D,))
    g_ln2 = din("L0_attn_ln2_g", (D,))
    b_ln2 = din("L0_attn_ln2_b", (D,))
    w_1 = din("L0_attn_w1", (HID, D))
    b_1 = din("L0_attn_b1", (HID,))
    w_2 = din("L0_attn_w2", (D, HID))
    b_2 = din("L0_attn_b2", (D,))
    w_rout = din("L0_ffn_routing_W", (NE, D))
    b_eb = din("L0_ffn_expert_bias", (NE,))
    w_sw1 = din("L0_ffn_sW1", (D, HE))
    b_sb1 = din("L0_ffn_sb1", (HE,))
    w_sw2 = din("L0_ffn_sW2", (HE, D))
    b_sb2 = din("L0_ffn_sb2", (D,))
    w_rw1 = din("L0_ffn_rW1", (NE, D, HE))
    b_rb1 = din("L0_ffn_rb1", (NE, HE))
    w_rw2 = din("L0_ffn_rW2", (NE, HE, D))
    b_rb2 = din("L0_ffn_rb2", (NE, D))
    ident_h = din("ident", (128, 128))
    sel_h = din("sel16", (16, 16 * 128))

    out_h = nc.dram_tensor("out", [T, D], F32, kind="ExternalOutput").ap()

    s_inproj = nc.dram_tensor("s_inproj", [16, 128, 2 * D], BF16, kind="Internal").ap()
    s_outproj = nc.dram_tensor("s_outproj", [8, 128, 2 * D], BF16, kind="Internal").ap()
    s_w1 = nc.dram_tensor("s_w1", [32, 128, 2 * D], BF16, kind="Internal").ap()
    s_w2 = nc.dram_tensor("s_w2", [8, 128, 2 * HID], BF16, kind="Internal").ap()
    s_sw1 = nc.dram_tensor("s_sw1", [4, 128, 2 * D], BF16, kind="Internal").ap()
    s_sw2 = nc.dram_tensor("s_sw2", [8, 128, 2 * HE], BF16, kind="Internal").ap()
    s_rw1 = nc.dram_tensor("s_rw1", [NE, 4, 128, 2 * D], BF16, kind="Internal").ap()
    s_rw2 = nc.dram_tensor("s_rw2", [NE, 8, 128, 2 * HE], BF16, kind="Internal").ap()

    taps = {}
    if debug:
        for nm, w in [("xnT", NKT * TC), ("x1T", NKT * TC), ("zs", NKT * TC),
                      ("xcT", NKT * TC), ("dlt", NKT * TC), ("yscan", NKT * TC),
                      ("ygate", NKT * TC), ("x2T", NKT * TC), ("h2", NKT * TC),
                      ("sT", NKT * TC), ("shared", NKT * TC), ("routed", NKT * TC),
                      ("g", NKT * TC), ("act", 16 * TC), ("dtS", TC), ("BS", TC), ("CS", TC)]:
            taps[nm] = nc.dram_tensor("tap_" + nm, [NCH, 128, w], F32,
                                      kind="ExternalOutput").ap()
    b_taps = Buf()

    prog = Prog()
    es = contextlib.ExitStack()

    def sb(name, p, f, dt=F32):
        t = es.enter_context(nc.sbuf_tensor(name, [p, f], dt))
        return t, Buf()

    idt, b_idt = sb("idt", 128, 128)
    sel, b_sel = sb("sel", 16, 16 * 128)
    g1bc, b_g1bc = sb("g1bc", 128, D)
    b1bc, b_b1bc = sb("b1bc", 128, D)
    onesrow, b_onesrow = sb("onesrow", 1, 128)
    onecolD, b_onecolD = sb("onecolD", 128, 1)
    colp, b_colp = sb("colp", 128, 8 * 7)
    C_LN2G, C_LN2B, C_CONVB, C_DTB, C_DP, C_B2, C_SB2 = range(7)

    def col(which, j):
        return colp[:, which * 8 + j: which * 8 + j + 1]

    sb1c, b_sb1c = sb("sb1c", 128, 4)
    b1c, b_b1c = sb("b1c", 128, 32)
    rb1c, b_rb1c = sb("rb1c", 128, 32)
    ebc, b_ebc = sb("ebc", 8, 1)
    aneg, b_aneg = sb("aneg", 128, 128)
    convw, b_convw = sb("convw", 128, 64)
    xprojT, b_xprojT = sb("xprojT", 128, 8 * 96)
    dtprojT, b_dtprojT = sb("dtprojT", 64, D)
    routT, b_routT = sb("routT", 128, 64)
    routHL, b_routHL = sb("routHL", 128, 128, BF16)
    rb2t, b_rb2t = sb("rb2t", 8, D)
    stage, b_stage = sb("stage", 128, 128)
    rowstage, b_rowstage = sb("rowstage", 32, 128)
    grow, b_grow = sb("grow", 1, D)
    brow, b_brow = sb("brow", 1, D)

    xb, b_xb = sb("xb", 128, 2 * D)
    xnH, b_xnH = sb("xnH", 128, NKT * TC, BF16)
    xnL, b_xnL = sb("xnL", 128, NKT * TC, BF16)
    x1H, b_x1H = sb("x1H", 128, NKT * TC, BF16)
    x1L, b_x1L = sb("x1L", 128, NKT * TC, BF16)
    h2H, b_h2H = sb("h2H", 128, NKT * TC, BF16)
    h2L, b_h2L = sb("h2L", 128, NKT * TC, BF16)
    f32a, b_f32a = sb("f32a", 128, TC)
    f32b, b_f32b = sb("f32b", 128, TC)
    x1y, b_x1y = sb("x1y", 128, NKT * TC)
    zsb, b_zsb = sb("zsb", 128, NKT * TC)
    xinT, b_xinT = sb("xinT", 128, NKT * (TC + 7))
    xcT, b_xcT = sb("xcT", 128, NKT * TC)
    dlt, b_dlt = sb("dlt", 128, NKT * TC)
    sT, b_sT = dlt, b_dlt
    dxc, b_dxc = sb("dxc", 128, NKT * TC)
    x2T, b_x2T = sb("x2T", 128, NKT * TC)
    shg, b_shg = sb("shg", 128, NKT * TC)
    lt1, b_lt1 = sb("lt1", 128, D)
    Bbs, b_Bbs = sb("Bbs", 128, TC)
    dtS, b_dtS = sb("dtS", 64, TC)
    BS, b_BS = sb("BS", 16, TC)
    CS, b_CS = sb("CS", 16, TC)
    scrH, b_scrH = sb("scrH", 128, 16 * TC, BF16)
    scrL, b_scrL = sb("scrL", 128, 16 * TC, BF16)
    dAb = es.enter_context(nc.sbuf_tensor("dAb", [128, 2 * TC], F32))
    b_dAb = [Buf(), Buf()]
    dBxb = es.enter_context(nc.sbuf_tensor("dBxb", [128, 2 * TC], F32))
    b_dBxb = [Buf(), Buf()]
    hbb = es.enter_context(nc.sbuf_tensor("hbb", [128, 4 * TC], F32))
    b_hbb = [Buf() for _ in range(4)]
    hCb = es.enter_context(nc.sbuf_tensor("hCb", [128, 2 * TC], F32))
    b_hCb = [Buf(), Buf()]
    carry, b_carry = sb("carry", 128, 128)
    expT, b_expT = sb("expT", 128, TC)
    actS, b_actS = sb("actS", 8, TC)
    actT1, b_actT1 = sb("actT1", 8, TC)
    a_tr, b_a_tr = sb("a_tr", 128, 16)
    tmp8, b_tmp8 = sb("tmp8", 128, 16)
    cfb, b_cfb = sb("cfb", 128, 16)
    cfT, b_cfT = sb("cfT", 8, TC)
    stats, b_stats = sb("stats", 128, 32)
    rowS, b_rowS = sb("rowS", 1, 4 * TC)
    epsc, b_epsc = sb("epsc", 128, 1)
    jbuf, b_jbuf = sb("jbuf", 128, 256)
    stgs = []
    for si in range(4):
        stgs.append(sb(f"stg{si}", 128, 2048, BF16))
    stgf = []
    for si in range(2):
        stgf.append(sb(f"stgf{si}", 128, 1024))
    stgf_i = [0]

    PB = []
    b_pb = []
    for i in range(8):
        t = es.enter_context(nc.psum_tensor(f"pb{i}", [128, 512], F32))
        PB.append(t)
        b_pb.append(Buf())

    rot = Rot()

    b_scratch = {k: Buf() for k in ["inproj", "outproj", "w1", "w2", "sw1", "sw2", "rw1", "rw2"]}
    b_out = Buf()

    def dma(eng, q, out_ap, in_ap, reads=(), writes=(), extra=()):
        return ex(prog, eng, lambda e, o=out_ap, i=in_ap: e.dma_start(out=o, in_=i),
                  reads=reads, writes=writes, q=q, extra=extra)

    last_chain = {}

    def dma_chained(eng, q, out_ap, in_ap, reads=(), writes=()):
        extra = [last_chain.get(q)]
        tok = dma(eng, q, out_ap, in_ap, reads=reads, writes=writes, extra=extra)
        last_chain[q] = tok
        return tok

    def top(eng, fname, *args, reads=(), writes=(), **kw):
        def fn(e, fname=fname, args=args, kw=kw):
            return getattr(e, fname)(*args, **kw)
        return ex(prog, eng, fn, reads=reads, writes=writes)

    stg_i = [0]

    def next_stg():
        stg_i[0] = (stg_i[0] + 1) % 4
        t, b = stgs[stg_i[0]]
        return t, b, stg_i[0]

    def mm_acc(slot_view, slot_buf, pairs):
        n = len(pairs)
        for i, (sap, sbufs, map_, mbufs) in enumerate(pairs):
            def fn(e, sap=sap, map_=map_, st=(i == 0), sp=(i == n - 1), sv=slot_view):
                return e.matmul(sv, sap, map_, start=st, stop=sp)
            ex(prog, "tensor", fn, reads=tuple(sbufs) + tuple(mbufs), writes=(slot_buf,))

    def mm3_acc(slot_view, slot_buf, trips):
        n = 3 * len(trips)
        i = 0
        for (sh, sl, sbufs, mh, ml, mbufs) in trips:
            for sap, map_ in ((sh, mh), (sh, ml), (sl, mh)):
                def fn(e, sap=sap, map_=map_, st=(i == 0), sp=(i == n - 1), sv=slot_view):
                    return e.matmul(sv, sap, map_, start=st, stop=sp)
                ex(prog, "tensor", fn, reads=tuple(sbufs) + tuple(mbufs), writes=(slot_buf,))
                i += 1

    dma_chained("sync", "qc", idt[:], ident_h[:], writes=(b_idt,))
    dma_chained("sync", "qc", sel[:], sel_h[:], writes=(b_sel,))
    top("vector", "memset", onesrow[:], 1.0, writes=(b_onesrow,))
    top("vector", "memset", onecolD[:], 1.0 / D, writes=(b_onecolD,))
    top("vector", "memset", epsc[:], EPS, writes=(b_epsc,))

    dma_chained("sync", "qc", grow[:], g_ln1.rearrange("(a d) -> a d", a=1), writes=(b_grow,))
    dma_chained("sync", "qc", brow[:], b_ln1.rearrange("(a d) -> a d", a=1), writes=(b_brow,))
    for half in range(2):
        s = rot.take()
        top("tensor", "matmul", PB[s][:, 0:512], onesrow[:], grow[:, half * 512:(half + 1) * 512],
            start=True, stop=True, reads=(b_onesrow, b_grow), writes=(b_pb[s],))
        top("vector", "tensor_copy", g1bc[:, half * 512:(half + 1) * 512], PB[s][:, 0:512],
            reads=(b_pb[s],), writes=(b_g1bc,))
        s = rot.take()
        top("tensor", "matmul", PB[s][:, 0:512], onesrow[:], brow[:, half * 512:(half + 1) * 512],
            start=True, stop=True, reads=(b_onesrow, b_brow), writes=(b_pb[s],))
        top("vector", "tensor_copy", b1bc[:, half * 512:(half + 1) * 512], PB[s][:, 0:512],
            reads=(b_pb[s],), writes=(b_b1bc,))

    def load_cols(vec_ap, which):
        dma_chained("sync", "qc", rowstage[0:8, :], vec_ap.rearrange("(k p) -> k p", p=128),
            reads=(b_rowstage,), writes=(b_rowstage,))
        s = rot.take()
        top("tensor", "transpose", PB[s][0:128, 0:8], rowstage[0:8, :], idt[0:8, 0:8],
            reads=(b_rowstage, b_idt), writes=(b_pb[s],))
        top("vector", "tensor_copy", colp[:, which * 8:(which + 1) * 8], PB[s][0:128, 0:8],
            reads=(b_pb[s],), writes=(b_colp,))

    load_cols(g_ln2, C_LN2G)
    load_cols(b_ln2, C_LN2B)
    load_cols(b_conv, C_CONVB)
    load_cols(b_dt, C_DTB)
    load_cols(d_param, C_DP)
    load_cols(b_2, C_B2)
    load_cols(b_sb2, C_SB2)

    dma_chained("sync", "qc", rowstage[0:4, :], b_sb1.rearrange("(k p) -> k p", p=128),
        reads=(b_rowstage,), writes=(b_rowstage,))
    s = rot.take()
    top("tensor", "transpose", PB[s][0:128, 0:4], rowstage[0:4, :], idt[0:4, 0:4],
        reads=(b_rowstage, b_idt), writes=(b_pb[s],))
    top("vector", "tensor_copy", sb1c[:], PB[s][0:128, 0:4], reads=(b_pb[s],), writes=(b_sb1c,))
    dma_chained("sync", "qc", rowstage[:], b_1.rearrange("(k p) -> k p", p=128),
        reads=(b_rowstage,), writes=(b_rowstage,))
    s = rot.take()
    top("tensor", "transpose", PB[s][0:128, 0:32], rowstage[:], idt[0:32, 0:32],
        reads=(b_rowstage, b_idt), writes=(b_pb[s],))
    top("vector", "tensor_copy", b1c[:], PB[s][0:128, 0:32], reads=(b_pb[s],), writes=(b_b1c,))
    dma_chained("sync", "qc", rowstage[:], b_rb1.rearrange("e (k p) -> (e k) p", p=128),
        reads=(b_rowstage,), writes=(b_rowstage,))
    s = rot.take()
    top("tensor", "transpose", PB[s][0:128, 0:32], rowstage[:], idt[0:32, 0:32],
        reads=(b_rowstage, b_idt), writes=(b_pb[s],))
    top("vector", "tensor_copy", rb1c[:], PB[s][0:128, 0:32], reads=(b_pb[s],), writes=(b_rb1c,))
    dma_chained("sync", "qc", ebc[:], b_eb.rearrange("(p a) -> p a", a=1), writes=(b_ebc,))
    for db in range(NKT):
        dma_chained("sync", "qc", stage[:, 0:16], a_log[db * 128:(db + 1) * 128, :],
            reads=(b_stage,), writes=(b_stage,))
        top("scalar", "activation", aneg[:, db * 16:(db + 1) * 16], stage[:, 0:16], AF.Exp,
            reads=(b_stage,), writes=(b_aneg,))
        top("vector", "tensor_scalar_mul", aneg[:, db * 16:(db + 1) * 16],
            aneg[:, db * 16:(db + 1) * 16], -1.0, reads=(b_aneg,), writes=(b_aneg,))
    for db in range(NKT):
        dma_chained("sync", "qc", convw[:, db * 8:(db + 1) * 8], w_conv[db * 128:(db + 1) * 128, :],
            reads=(b_convw,), writes=(b_convw,))
    for kt in range(NKT):
        dma_chained("sync", "qc", stage[0:96, :], w_xproj[:, kt * 128:(kt + 1) * 128],
            reads=(b_stage,), writes=(b_stage,))
        s = rot.take()
        top("tensor", "transpose", PB[s][0:128, 0:96], stage[0:96, :], idt[0:96, 0:96],
            reads=(b_stage, b_idt), writes=(b_pb[s],))
        top("vector", "tensor_copy", xprojT[:, kt * 96:(kt + 1) * 96], PB[s][0:128, 0:96],
            reads=(b_pb[s],), writes=(b_xprojT,))
    for m in range(NKT):
        dma_chained("sync", "qc", stage[:, 0:64], w_dtproj[m * 128:(m + 1) * 128, :],
            reads=(b_stage,), writes=(b_stage,))
        s = rot.take()
        top("tensor", "transpose", PB[s][0:64, 0:128], stage[:, 0:64], idt[:],
            reads=(b_stage, b_idt), writes=(b_pb[s],))
        top("vector", "tensor_copy", dtprojT[:, m * 128:(m + 1) * 128], PB[s][0:64, 0:128],
            reads=(b_pb[s],), writes=(b_dtprojT,))
    for kt in range(NKT):
        dma_chained("sync", "qc", stage[0:8, :], w_rout[:, kt * 128:(kt + 1) * 128],
            reads=(b_stage,), writes=(b_stage,))
        s = rot.take()
        top("tensor", "transpose", PB[s][0:128, 0:8], stage[0:8, :], idt[0:8, 0:8],
            reads=(b_stage, b_idt), writes=(b_pb[s],))
        top("vector", "tensor_copy", routT[:, kt * 8:(kt + 1) * 8], PB[s][0:128, 0:8],
            reads=(b_pb[s],), writes=(b_routT,))
    dma_chained("sync", "qc", rb2t[:], b_rb2[:], writes=(b_rb2t,))
    top("scalar", "copy", routHL[:, 0:64], routT[:], reads=(b_routT,), writes=(b_routHL,))
    top("vector", "tensor_tensor", routHL[:, 64:128], routT[:], routHL[:, 0:64],
        OP.subtract, reads=(b_routT,), writes=(b_routHL,))

    def phase0_transpose(w_ap, n_r, n_c, scratch_ap, sbuf_key, split=True, q="qa2"):
        bscr = b_scratch[sbuf_key]
        K = n_c * 128
        for r in range(n_r):
            for cg0 in range(0, n_c, 8):
                cg1 = min(cg0 + 8, n_c)
                stgf_i[0] ^= 1
                w_stg, b_wstg = stgf[stgf_i[0]]
                dma("sync", f"qf{stgf_i[0]}", w_stg[:, 0:(cg1 - cg0) * 128],
                    w_ap[r * 128:(r + 1) * 128, cg0 * 128:cg1 * 128],
                    reads=(b_wstg,), writes=(b_wstg,))
                o_stg, b_ostg, _osi = next_stg()
                ow = (cg1 - cg0) * 128
                for c in range(cg0, cg1):
                    s = rot.take()
                    j = (c - cg0) * 128
                    top("tensor", "transpose", PB[s][0:128, 0:128],
                        w_stg[:, j:j + 128], idt[:],
                        reads=(b_wstg, b_idt), writes=(b_pb[s],))
                    if split:
                        top("scalar", "copy", o_stg[:, j:j + 128], PB[s][0:128, 0:128],
                            reads=(b_pb[s],), writes=(b_ostg,))
                        top("vector", "tensor_tensor", o_stg[:, ow + j:ow + j + 128],
                            PB[s][0:128, 0:128], o_stg[:, j:j + 128], OP.subtract,
                            reads=(b_pb[s],), writes=(b_ostg,))
                    else:
                        top("vector", "tensor_copy", o_stg[:, j:j + 128],
                            PB[s][0:128, 0:128], reads=(b_pb[s],), writes=(b_ostg,))
                if split:
                    dma("gpsimd", q, scratch_ap[r, :, cg0 * 128:cg1 * 128],
                        o_stg[:, 0:ow], reads=(b_ostg,), writes=(bscr,))
                    dma("gpsimd", q, scratch_ap[r, :, K + cg0 * 128:K + cg1 * 128],
                        o_stg[:, ow:2 * ow], reads=(b_ostg,), writes=(bscr,))
                else:
                    dma("gpsimd", q, scratch_ap[r, :, cg0 * 128:cg1 * 128],
                        o_stg[:, 0:ow], reads=(b_ostg,), writes=(bscr,))

    def phase0_reorder(v_ap, n_kt, n_m, scratch_ap, sbuf_key, width, split=True):
        bscr = b_scratch[sbuf_key]
        v3 = v_ap.rearrange("(kt p) o -> p kt o", p=128)
        for m in range(n_m):
            stgf_i[0] ^= 1
            w_stg, b_wstg = stgf[stgf_i[0]]
            dma("sync", f"qf{stgf_i[0]}",
                w_stg[:, 0:n_kt * 128].rearrange("p (k j) -> p k j", j=128),
                v3[:, :, m * 128:(m + 1) * 128],
                reads=(b_wstg,), writes=(b_wstg,))
            if split:
                o_stg, b_ostg, osi = next_stg()
                top("scalar", "copy", o_stg[:, 0:width], w_stg[:, 0:width],
                    reads=(b_wstg,), writes=(b_ostg,))
                top("vector", "tensor_tensor", o_stg[:, width:2 * width],
                    w_stg[:, 0:width], o_stg[:, 0:width], OP.subtract,
                    reads=(b_wstg,), writes=(b_ostg,))
                dma("gpsimd", "qa2", scratch_ap[m, :, 0:2 * width], o_stg[:, 0:2 * width],
                    reads=(b_ostg,), writes=(bscr,))
            else:
                dma("gpsimd", "qa2", scratch_ap[m, :, 0:width], w_stg[:, 0:width],
                    reads=(b_wstg,), writes=(bscr,))

    def phase0_transpose_f32(w_ap, n_r, n_c, scratch_ap, sbuf_key):
        bscr = b_scratch[sbuf_key]
        for r in range(n_r):
            stgf_i[0] ^= 1
            w_stg, b_wstg = stgf[stgf_i[0]]
            dma("sync", f"qf{stgf_i[0]}", w_stg[:, 0:n_c * 128],
                w_ap[r * 128:(r + 1) * 128, 0:n_c * 128],
                reads=(b_wstg,), writes=(b_wstg,))
            stgf_i[0] ^= 1
            o_stg, b_ostg = stgf[stgf_i[0]]
            for c in range(n_c):
                s = rot.take()
                top("tensor", "transpose", PB[s][0:128, 0:128],
                    w_stg[:, c * 128:(c + 1) * 128], idt[:],
                    reads=(b_wstg, b_idt), writes=(b_pb[s],))
                top("vector", "tensor_copy", o_stg[:, c * 128:(c + 1) * 128],
                    PB[s][0:128, 0:128], reads=(b_pb[s],), writes=(b_ostg,))
            dma("gpsimd", "qa2", scratch_ap[r, :, 0:n_c * 128], o_stg[:, 0:n_c * 128],
                reads=(b_ostg,), writes=(bscr,))

    phase0_transpose(w_inproj, 16, 8, s_inproj, "inproj", split=True, q="qa1")
    phase0_transpose(w_outproj, 8, 8, s_outproj, "outproj", split=True)
    phase0_transpose(w_1, 32, 8, s_w1, "w1", split=True)
    phase0_transpose(w_2, 8, 32, s_w2, "w2", split=True)
    phase0_reorder(w_sw1, 8, 4, s_sw1, "sw1", D, split=True)
    phase0_reorder(w_sw2, 4, 8, s_sw2, "sw2", HE, split=True)
    for e in range(NE):
        phase0_reorder(w_rw1[e], 8, 4, s_rw1[e], "rw1", D, split=True)
        phase0_reorder(w_rw2[e], 4, 8, s_rw2[e], "rw2", HE, split=True)

    for key, bscr in b_scratch.items():
        qn = "qa1" if key == "inproj" else "qa2"
        bscr.w = (qn, prog.cnt.get(qn, 0))
        bscr.r = []

    for db in range(NKT):
        top("vector", "memset", xinT[:, db * (TC + 7): db * (TC + 7) + 7], 0.0,
            writes=(b_xinT,))

    SRS, SRQ, SMEAN, SMQ, SM2, SVAR, SRSTD = range(7)

    def stc(tt, which, ln):
        j = tt * 16 + ln * 8 + which
        return stats[:, j:j + 1]

    def weight_load(scratch_ap, m, width, key):
        w_stg, b_wstg, si = next_stg()
        dma("sync", f"qw{si}", w_stg[:, 0:width], scratch_ap[m, :, 0:width],
            reads=(b_wstg, b_scratch[key]), writes=(b_wstg,))
        return w_stg, b_wstg

    def weight_load_f32(scratch_ap, m, width, key):
        stgf_i[0] ^= 1
        w_stg, b_wstg = stgf[stgf_i[0]]
        dma("sync", f"qf{stgf_i[0]}", w_stg[:, 0:width], scratch_ap[m, :, 0:width],
            reads=(b_wstg, b_scratch[key]), writes=(b_wstg,))
        return w_stg, b_wstg

    for c in range(NCH):
        t0 = c * TC

        def tap(nm, src_ap, src_buf, c=c):
            if debug:
                dma("gpsimd", "qt", taps[nm][c][0:src_ap.shape[0], 0:src_ap.shape[1]],
                    src_ap, reads=(src_buf,), writes=(b_taps,))

        for tt in range(NTT):
            xv = xb[:, tt * D:(tt + 1) * D]
            dma_chained("scalar", "qx", xv, x_h[t0 + tt * 128: t0 + (tt + 1) * 128, :],
                reads=(b_xb,), writes=(b_xb,))
        for tt in range(NTT):
            xv = xb[:, tt * D:(tt + 1) * D]
            sq = dxc[:, tt * D:(tt + 1) * D]
            for ln in range(2):
                top("vector", "reduce_sum", stc(tt, SRS, ln), xv, AX,
                    reads=(b_xb,), writes=(b_stats,))
                top("scalar", "activation", sq, xv, AF.Square,
                    reads=(b_xb,), writes=(b_dxc,))
                top("vector", "reduce_sum", stc(tt, SRQ, ln), sq, AX,
                    reads=(b_dxc,), writes=(b_stats,))
                top("scalar", "mul", stc(tt, SMEAN, ln), stc(tt, SRS, ln), 1.0 / D,
                    reads=(b_stats,), writes=(b_stats,))
                top("scalar", "mul", stc(tt, SMQ, ln), stc(tt, SRQ, ln), 1.0 / D,
                    reads=(b_stats,), writes=(b_stats,))
                top("vector", "scalar_tensor_tensor", stc(tt, SVAR, ln), stc(tt, SMEAN, ln),
                    stc(tt, SMEAN, ln), stc(tt, SMQ, ln), OP.mult, OP.subtract,
                    reads=(b_stats,), writes=(b_stats,))
                top("scalar", "activation", stc(tt, SM2, ln), stc(tt, SVAR, ln), AF.Ln,
                    scale=-1.0, bias=epsc[:], reads=(b_stats, b_epsc), writes=(b_stats,))
                top("scalar", "activation", stc(tt, SRSTD, ln), stc(tt, SM2, ln), AF.Exp,
                    scale=-0.5, reads=(b_stats,), writes=(b_stats,))
                if ln == 0:
                    top("vector", "tensor_scalar", xv, xv, stc(tt, SMEAN, 0), stc(tt, SRSTD, 0),
                        OP.subtract, OP.mult, reads=(b_stats, b_xb), writes=(b_xb,))
                    for kt in range(NKT):
                        s = rot.take()
                        cc = kt * TC + tt * 128
                        top("tensor", "transpose", PB[s][0:128, 0:128],
                            xv[:, kt * 128:(kt + 1) * 128], idt[:],
                            reads=(b_xb, b_idt), writes=(b_pb[s],))
                        top("scalar", "copy", xnH[:, cc:cc + 128], PB[s][0:128, 0:128],
                            reads=(b_pb[s],), writes=(b_xnH,))
                        top("vector", "tensor_tensor", xnL[:, cc:cc + 128],
                            PB[s][0:128, 0:128], xnH[:, cc:cc + 128], OP.subtract,
                            reads=(b_pb[s], b_xnH), writes=(b_xnL,))
                else:
                    top("vector", "tensor_scalar", lt1[:], xv, stc(tt, SMEAN, 1), stc(tt, SRSTD, 1),
                        OP.subtract, OP.mult, reads=(b_stats, b_xb), writes=(b_lt1,))
                    top("vector", "tensor_tensor", lt1[:], lt1[:], g1bc[:], OP.mult,
                        reads=(b_g1bc,), writes=(b_lt1,))
                    top("vector", "tensor_tensor", lt1[:], lt1[:], b1bc[:], OP.add,
                        reads=(b_b1bc,), writes=(b_lt1,))
                    for kt in range(NKT):
                        s = rot.take()
                        cc = kt * TC + tt * 128
                        top("tensor", "transpose", PB[s][0:128, 0:128],
                            lt1[:, kt * 128:(kt + 1) * 128], idt[:],
                            reads=(b_lt1, b_idt), writes=(b_pb[s],))
                        top("scalar", "copy", x1H[:, cc:cc + 128], PB[s][0:128, 0:128],
                            reads=(b_pb[s],), writes=(b_x1H,))
                        top("vector", "tensor_tensor", x1L[:, cc:cc + 128],
                            PB[s][0:128, 0:128], x1H[:, cc:cc + 128], OP.subtract,
                            reads=(b_pb[s], b_x1H), writes=(b_x1L,))

        def conv_db(db):
            base = db * (TC + 7)
            acc = dxc[:, db * TC:(db + 1) * TC]
            top("vector", "tensor_scalar", acc, xinT[:, base: base + TC],
                convw[:, db * 8: db * 8 + 1], None, OP.mult,
                reads=(b_xinT, b_convw), writes=(b_dxc,))
            for k in range(1, DCONV):
                top("vector", "scalar_tensor_tensor", acc, xinT[:, base + k: base + k + TC],
                    convw[:, db * 8 + k: db * 8 + k + 1], acc, OP.mult, OP.add,
                    reads=(b_xinT, b_convw), writes=(b_dxc,))
            top("scalar", "activation", xcT[:, db * TC:(db + 1) * TC], acc, AF.Silu,
                bias=col(C_CONVB, db), reads=(b_dxc, b_colp), writes=(b_xcT,))

        for m in range(16):
            w_stg, b_wstg = weight_load(s_inproj, m, 2 * D, "inproj")
            s = rot.take()
            mm3_acc(PB[s][:, 0:TC], b_pb[s],
                    [(w_stg[:, kt * 128:(kt + 1) * 128],
                      w_stg[:, D + kt * 128: D + (kt + 1) * 128], (b_wstg,),
                      x1H[:, kt * TC:(kt + 1) * TC], x1L[:, kt * TC:(kt + 1) * TC],
                      (b_x1H, b_x1L)) for kt in range(NKT)])
            if m < 8:
                base = m * (TC + 7)
                top("scalar", "copy", xinT[:, base + 7: base + 7 + TC], PB[s][:, 0:TC],
                    reads=(b_pb[s],), writes=(b_xinT,))
            else:
                top("scalar", "activation", zsb[:, (m - 8) * TC:(m - 7) * TC], PB[s][:, 0:TC],
                    AF.Silu, reads=(b_pb[s],), writes=(b_zsb,))
                conv_db(m - 8)
        tap("zs", zsb[:], b_zsb)
        tap("xcT", xcT[:], b_xcT)
        for db in range(NKT):
            base = db * (TC + 7)
            top("vector", "tensor_copy", xinT[:, base: base + 7],
                xinT[:, base + TC: base + TC + 7], reads=(b_xinT,), writes=(b_xinT,))

        s = rot.take()
        mm3_acc(PB[s][0:8, 0:TC], b_pb[s],
                [(routHL[:, kt * 8:(kt + 1) * 8], routHL[:, 64 + kt * 8: 64 + (kt + 1) * 8],
                  (b_routHL,),
                  xnH[:, kt * TC:(kt + 1) * TC], xnL[:, kt * TC:(kt + 1) * TC],
                  (b_xnH, b_xnL)) for kt in range(NKT)])
        top("scalar", "activation", actT1[:], PB[s][0:8, 0:TC], AF.Exp, bias=ebc[:],
            reads=(b_pb[s], b_ebc), writes=(b_actT1,))
        top("scalar", "activation", actT1[:], actT1[:], AF.Ln, bias=1.0,
            reads=(b_actT1,), writes=(b_actT1,))
        top("scalar", "activation", actT1[:], actT1[:], AF.Ln,
            reads=(b_actT1,), writes=(b_actT1,))
        top("scalar", "activation", actS[:], actT1[:], AF.Exp, scale=0.5,
            reads=(b_actT1,), writes=(b_actS,))
        for tt in range(NTT):
            s = rot.take()
            top("tensor", "transpose", PB[s][0:128, 0:8], actS[:, tt * 128:(tt + 1) * 128],
                idt[0:8, 0:8], reads=(b_actS, b_idt), writes=(b_pb[s],))
            top("vector", "tensor_copy", a_tr[:, tt * 8:(tt + 1) * 8], PB[s][0:128, 0:8],
                reads=(b_pb[s],), writes=(b_a_tr,))
        def vjunk():
            top("vector", "tensor_copy", jbuf[:], g1bc[:, 0:256],
                reads=(b_g1bc,), writes=(b_jbuf,))

        vjunk()
        for tt in range(NTT):
            top("vector", "reduce_max", stc(tt, 0, 0), a_tr[:, tt * 8:(tt + 1) * 8], AX,
                reads=(b_a_tr,), writes=(b_stats,))
        vjunk()
        for tt in range(NTT):
            top("vector", "tensor_scalar", tmp8[:, tt * 8:(tt + 1) * 8],
                a_tr[:, tt * 8:(tt + 1) * 8], stc(tt, 0, 0), None, OP.is_equal,
                reads=(b_a_tr, b_stats), writes=(b_tmp8,))
        vjunk()
        for tt in range(NTT):
            top("vector", "scalar_tensor_tensor", tmp8[:, tt * 8:(tt + 1) * 8],
                tmp8[:, tt * 8:(tt + 1) * 8], -BIG, a_tr[:, tt * 8:(tt + 1) * 8],
                OP.mult, OP.add, reads=(b_a_tr,), writes=(b_tmp8,))
        vjunk()
        for tt in range(NTT):
            top("vector", "reduce_max", stc(tt, 1, 0), tmp8[:, tt * 8:(tt + 1) * 8], AX,
                reads=(b_tmp8,), writes=(b_stats,))
        vjunk()
        for tt in range(NTT):
            top("vector", "tensor_scalar", cfb[:, tt * 8:(tt + 1) * 8],
                a_tr[:, tt * 8:(tt + 1) * 8], stc(tt, 1, 0), None, OP.is_ge,
                reads=(b_a_tr, b_stats), writes=(b_cfb,))
        vjunk()
        for tt in range(NTT):
            top("vector", "tensor_tensor", cfb[:, tt * 8:(tt + 1) * 8],
                cfb[:, tt * 8:(tt + 1) * 8], a_tr[:, tt * 8:(tt + 1) * 8], OP.mult,
                reads=(b_a_tr,), writes=(b_cfb,))
        vjunk()
        for tt in range(NTT):
            s = rot.take()
            top("tensor", "transpose", PB[s][0:8, 0:128], cfb[:, tt * 8:(tt + 1) * 8], idt[:],
                reads=(b_cfb, b_idt), writes=(b_pb[s],))
            top("vector", "tensor_copy", cfT[:, tt * 128:(tt + 1) * 128], PB[s][0:8, 0:128],
                reads=(b_pb[s],), writes=(b_cfT,))

        tap("act", actS[:], b_actS)
        if debug:
            dma("gpsimd", "qt", taps["act"][c][8:16, 0:TC], cfT[:],
                reads=(b_cfT,), writes=(b_taps,))

        for (p0, p1, dst, b_dst) in [(0, 64, dtS, b_dtS), (64, 80, BS, b_BS), (80, 96, CS, b_CS)]:
            s = rot.take()
            mm_acc(PB[s][0:p1 - p0, 0:TC], b_pb[s],
                   [(xprojT[:, kt * 96 + p0: kt * 96 + p1], (b_xprojT,),
                     xcT[:, kt * TC:(kt + 1) * TC], (b_xcT,)) for kt in range(NKT)])
            top("scalar", "copy", dst[:], PB[s][0:p1 - p0, 0:TC], reads=(b_pb[s],), writes=(b_dst,))
        for m in range(NKT):
            s = rot.take()
            mm_acc(PB[s][:, 0:TC], b_pb[s],
                   [(dtprojT[:, m * 128:(m + 1) * 128], (b_dtprojT,), dtS[:], (b_dtS,))])
            top("scalar", "activation", expT[:], PB[s][:, 0:TC], AF.Exp,
                bias=col(C_DTB, m), reads=(b_pb[s], b_colp), writes=(b_expT,))
            top("scalar", "activation", dlt[:, m * TC:(m + 1) * TC], expT[:], AF.Ln,
                bias=1.0, reads=(b_expT,), writes=(b_dlt,))
        for kt in range(NKT):
            top("vector", "tensor_tensor", dxc[:, kt * TC:(kt + 1) * TC],
                dlt[:, kt * TC:(kt + 1) * TC], xcT[:, kt * TC:(kt + 1) * TC], OP.mult,
                reads=(b_dlt, b_xcT), writes=(b_dxc,))

        tap("dlt", dlt[:], b_dlt)
        tap("dtS", dtS[:], b_dtS)
        tap("BS", BS[:], b_BS)
        tap("CS", CS[:], b_CS)

        def moe_steps():
            for m in range(4):
                w_stg, b_wstg = weight_load(s_sw1, m, 2 * D, "sw1")
                s = rot.take()
                mm3_acc(PB[s][:, 0:TC], b_pb[s],
                        [(w_stg[:, kt * 128:(kt + 1) * 128],
                          w_stg[:, D + kt * 128: D + (kt + 1) * 128], (b_wstg,),
                          xnH[:, kt * TC:(kt + 1) * TC], xnL[:, kt * TC:(kt + 1) * TC],
                          (b_xnH, b_xnL)) for kt in range(NKT)])
                top("scalar", "activation", lt1[:, m * TC:(m + 1) * TC], PB[s][:, 0:TC],
                    AF.Identity, bias=sb1c[:, m:m + 1],
                    reads=(b_pb[s], b_sb1c), writes=(b_lt1,))
                yield
            for m in range(4):
                top("scalar", "activation", lt1[:, m * TC:(m + 1) * TC],
                    lt1[:, m * TC:(m + 1) * TC], AF.Silu, reads=(b_lt1,), writes=(b_lt1,))
            for m in range(4):
                top("scalar", "copy", h2H[:, m * TC:(m + 1) * TC], lt1[:, m * TC:(m + 1) * TC],
                    reads=(b_lt1,), writes=(b_h2H,))
                top("vector", "tensor_tensor", h2L[:, m * TC:(m + 1) * TC],
                    lt1[:, m * TC:(m + 1) * TC], h2H[:, m * TC:(m + 1) * TC], OP.subtract,
                    reads=(b_lt1, b_h2H), writes=(b_h2L,))
            yield
            for m in range(NKT):
                w_stg, b_wstg = weight_load(s_sw2, m, 2 * HE, "sw2")
                s = rot.take()
                mm3_acc(PB[s][:, 0:TC], b_pb[s],
                        [(w_stg[:, kt * 128:(kt + 1) * 128],
                          w_stg[:, HE + kt * 128: HE + (kt + 1) * 128], (b_wstg,),
                          h2H[:, kt * TC:(kt + 1) * TC], h2L[:, kt * TC:(kt + 1) * TC],
                          (b_h2H, b_h2L)) for kt in range(4)])
                top("scalar", "activation", shg[:, m * TC:(m + 1) * TC], PB[s][:, 0:TC],
                    AF.Identity, bias=col(C_SB2, m), reads=(b_pb[s], b_colp), writes=(b_shg,))
                yield
            f32x_i = [0]

            def f32x():
                f32x_i[0] ^= 1
                return (f32a, b_f32a) if f32x_i[0] == 0 else (f32b, b_f32b)

            for grp in range(2):
                for ei in range(4):
                    e = grp * 4 + ei
                    for mh in range(4):
                        w_stg, b_wstg = weight_load_rw1(e, mh)
                        s = rot.take()
                        mm3_acc(PB[s][:, 0:TC], b_pb[s],
                                [(w_stg[:, kt * 128:(kt + 1) * 128],
                                  w_stg[:, D + kt * 128: D + (kt + 1) * 128], (b_wstg,),
                                  xnH[:, kt * TC:(kt + 1) * TC], xnL[:, kt * TC:(kt + 1) * TC],
                                  (b_xnH, b_xnL)) for kt in range(NKT)])
                        j = ei * 4 + mh
                        top("scalar", "activation", scrH[:, j * TC:(j + 1) * TC],
                            PB[s][:, 0:TC], AF.Identity,
                            bias=rb1c[:, e * 4 + mh: e * 4 + mh + 1],
                            reads=(b_pb[s], b_rb1c), writes=(b_scrH,))
                        top("vector", "scalar_tensor_tensor", scrL[:, j * TC:(j + 1) * TC],
                            PB[s][:, 0:TC], rb1c[:, e * 4 + mh: e * 4 + mh + 1],
                            scrH[:, j * TC:(j + 1) * TC], OP.add, OP.subtract,
                            reads=(b_pb[s], b_rb1c, b_scrH), writes=(b_scrL,))
                        yield
                for ei in range(4):
                    e = grp * 4 + ei
                    s2 = rot.take()
                    top("tensor", "matmul", PB[s2][:, 0:TC], sel[0:8, e * 128:(e + 1) * 128],
                        cfT[:], start=True, stop=True, reads=(b_sel, b_cfT), writes=(b_pb[s2],))
                    for mh in range(4):
                        j = ei * 4 + mh
                        fx, b_fx = f32x()
                        top("vector", "tensor_tensor", fx[:], scrH[:, j * TC:(j + 1) * TC],
                            scrL[:, j * TC:(j + 1) * TC], OP.add,
                            reads=(b_scrH, b_scrL), writes=(b_fx,))
                        top("scalar", "activation", fx[:], fx[:], AF.Silu,
                            reads=(b_fx,), writes=(b_fx,))
                        top("vector", "tensor_tensor", fx[:], fx[:], PB[s2][:, 0:TC],
                            OP.mult, reads=(b_pb[s2],), writes=(b_fx,))
                        top("scalar", "copy", scrH[:, j * TC:(j + 1) * TC], fx[:],
                            reads=(b_fx,), writes=(b_scrH,))
                        top("vector", "tensor_tensor", scrL[:, j * TC:(j + 1) * TC],
                            fx[:], scrH[:, j * TC:(j + 1) * TC], OP.subtract,
                            reads=(b_fx, b_scrH), writes=(b_scrL,))
                    yield
                for m in range(NKT):
                    s = rot.take()
                    i3 = 0
                    for ei in range(4):
                        e = grp * 4 + ei
                        w_stg, b_wstg = weight_load_rw2(e, m)
                        for kh in range(4):
                            j = ei * 4 + kh
                            for sap, map_, mb in (
                                (w_stg[:, kh * 128:(kh + 1) * 128],
                                 scrH[:, j * TC:(j + 1) * TC], (b_scrH,)),
                                (w_stg[:, kh * 128:(kh + 1) * 128],
                                 scrL[:, j * TC:(j + 1) * TC], (b_scrL,)),
                                (w_stg[:, HE + kh * 128: HE + (kh + 1) * 128],
                                 scrH[:, j * TC:(j + 1) * TC], (b_scrH,)),
                            ):
                                def fn(e2, sap=sap, map_=map_, st=(i3 == 0),
                                       sp=(grp == 0 and i3 == 47), sv=PB[s][:, 0:TC]):
                                    return e2.matmul(sv, sap, map_, start=st, stop=sp)
                                ex(prog, "tensor", fn, reads=(b_wstg,) + tuple(mb),
                                   writes=(b_pb[s],))
                                i3 += 1
                            yield
                    if grp == 1:
                        def fnb(eng, sap=rb2t[0:8, m * 128:(m + 1) * 128], map_=cfT[:],
                                sv=PB[s][:, 0:TC]):
                            return eng.matmul(sv, sap, map_, start=False, stop=True)
                        ex(prog, "tensor", fnb, reads=(b_rb2t, b_cfT), writes=(b_pb[s],))
                        top("vector", "tensor_tensor", xb[:, m * TC:(m + 1) * TC],
                            xb[:, m * TC:(m + 1) * TC], PB[s][:, 0:TC], OP.add,
                            reads=(b_pb[s],), writes=(b_xb,))
                    else:
                        top("vector", "tensor_copy", xb[:, m * TC:(m + 1) * TC],
                            PB[s][:, 0:TC], reads=(b_pb[s],), writes=(b_xb,))
                    yield

        def weight_load_rw1(e, mh):
            return weight_load(s_rw1[e], mh, 2 * D, "rw1")

        def weight_load_rw2(e, m):
            return weight_load(s_rw2[e], m, 2 * HE, "rw2")

        def mh_flat(e, mh):
            return 0

        moe = moe_steps()
        moe_done = [False]

        def moe_step(k=1):
            for _ in range(k):
                if moe_done[0]:
                    return
                try:
                    next(moe)
                except StopIteration:
                    moe_done[0] = True
                    return

        for n in range(DST):
            sb_slot = 6
            sc_slot = 7
            top("tensor", "matmul", PB[sb_slot][:, 0:TC], sel[:, n * 128:(n + 1) * 128], BS[:],
                start=True, stop=True, reads=(b_sel, b_BS), writes=(b_pb[sb_slot],))
            top("tensor", "matmul", PB[sc_slot][:, 0:TC], sel[:, n * 128:(n + 1) * 128], CS[:],
                start=True, stop=True, reads=(b_sel, b_CS), writes=(b_pb[sc_slot],))
            for db in range(NKT):
                ti = n * 8 + db
                i = ti % 2
                dA = dAb[:, i * TC:(i + 1) * TC]
                dBx = dBxb[:, i * TC:(i + 1) * TC]
                hb = hbb[:, (db % 4) * TC:(db % 4 + 1) * TC]
                hC = hCb[:, i * TC:(i + 1) * TC]
                top("scalar", "activation", dA, dlt[:, db * TC:(db + 1) * TC], AF.Exp,
                    scale=aneg[:, db * 16 + n: db * 16 + n + 1],
                    reads=(b_dlt, b_aneg), writes=(b_dAb[i],))
                top("vector", "tensor_tensor", dBx, dxc[:, db * TC:(db + 1) * TC],
                    PB[sb_slot][:, 0:TC], OP.mult,
                    reads=(b_dxc, b_pb[sb_slot]), writes=(b_dBxb[i],))
                init = 0.0 if c == 0 else carry[:, ti:ti + 1]
                top("vector", "tensor_tensor_scan", hb, dA, dBx, init, OP.mult, OP.add,
                    reads=(b_dAb[i], b_dBxb[i], b_carry), writes=(b_hbb[db % 4],))
                top("vector", "tensor_tensor", hC, hb, PB[sc_slot][:, 0:TC], OP.mult,
                    reads=(b_hbb[db % 4], b_pb[sc_slot]), writes=(b_hCb[i],))
                if n == 0:
                    top("vector", "tensor_copy", x1y[:, db * TC:(db + 1) * TC], hC,
                        reads=(b_hCb[i],), writes=(b_x1y,))
                else:
                    top("vector", "tensor_tensor", x1y[:, db * TC:(db + 1) * TC],
                        x1y[:, db * TC:(db + 1) * TC], hC, OP.add,
                        reads=(b_hCb[i],), writes=(b_x1y,))
                if db % 4 == 3:
                    for db2 in range(db - 3, db + 1):
                        top("vector", "tensor_copy", carry[:, n * 8 + db2: n * 8 + db2 + 1],
                            hbb[:, (db2 % 4 + 1) * TC - 1:(db2 % 4 + 1) * TC],
                            reads=(b_hbb[db2 % 4],), writes=(b_carry,))
                moe_step(1)
            moe_step(1)
        moe_step(1000)
        tap("yscan", x1y[:], b_x1y)
        tap("shared", shg[:], b_shg)
        tap("routed", xb[:], b_xb)

        for kt in range(NKT):
            top("vector", "scalar_tensor_tensor", x1y[:, kt * TC:(kt + 1) * TC],
                xcT[:, kt * TC:(kt + 1) * TC], col(C_DP, kt), x1y[:, kt * TC:(kt + 1) * TC],
                OP.mult, OP.add, reads=(b_xcT, b_colp), writes=(b_x1y,))
        for kt in range(NKT):
            top("vector", "tensor_tensor", x1y[:, kt * TC:(kt + 1) * TC],
                x1y[:, kt * TC:(kt + 1) * TC], zsb[:, kt * TC:(kt + 1) * TC], OP.mult,
                reads=(b_zsb,), writes=(b_x1y,))
        for kt in range(NKT):
            top("scalar", "copy", x1H[:, kt * TC:(kt + 1) * TC], x1y[:, kt * TC:(kt + 1) * TC],
                reads=(b_x1y,), writes=(b_x1H,))
            top("vector", "tensor_tensor", x1L[:, kt * TC:(kt + 1) * TC],
                x1y[:, kt * TC:(kt + 1) * TC], x1H[:, kt * TC:(kt + 1) * TC], OP.subtract,
                reads=(b_x1y, b_x1H), writes=(b_x1L,))

        tap("ygate", x1y[:], b_x1y)

        for kt in range(NKT):
            top("vector", "tensor_tensor", xb[:, kt * TC:(kt + 1) * TC],
                shg[:, kt * TC:(kt + 1) * TC], xb[:, kt * TC:(kt + 1) * TC], OP.add,
                reads=(b_shg,), writes=(b_xb,))
        for kt in range(NKT):
            top("scalar", "activation", shg[:, kt * TC:(kt + 1) * TC],
                xb[:, kt * TC:(kt + 1) * TC], AF.Silu, reads=(b_xb,), writes=(b_shg,))

        for m in range(NKT):
            w_stg, b_wstg = weight_load(s_outproj, m, 2 * D, "outproj")
            s = rot.take()
            mm3_acc(PB[s][:, 0:TC], b_pb[s],
                    [(w_stg[:, kt * 128:(kt + 1) * 128],
                      w_stg[:, D + kt * 128: D + (kt + 1) * 128], (b_wstg,),
                      x1H[:, kt * TC:(kt + 1) * TC], x1L[:, kt * TC:(kt + 1) * TC],
                      (b_x1H, b_x1L)) for kt in range(NKT)])
            top("vector", "tensor_tensor", x2T[:, m * TC:(m + 1) * TC], PB[s][:, 0:TC],
                xnH[:, m * TC:(m + 1) * TC], OP.add,
                reads=(b_pb[s], b_xnH), writes=(b_x2T,))
            top("vector", "tensor_tensor", x2T[:, m * TC:(m + 1) * TC],
                x2T[:, m * TC:(m + 1) * TC], xnL[:, m * TC:(m + 1) * TC], OP.add,
                reads=(b_xnL,), writes=(b_x2T,))

        tap("x2T", x2T[:], b_x2T)

        for kt in range(NKT):
            top("scalar", "activation", dxc[:, kt * TC:(kt + 1) * TC],
                x2T[:, kt * TC:(kt + 1) * TC], AF.Square, reads=(b_x2T,), writes=(b_dxc,))
        sA = rot.take()
        mm_acc(PB[sA][0:1, 0:TC], b_pb[sA],
               [(onecolD[:], (b_onecolD,), x2T[:, kt * TC:(kt + 1) * TC], (b_x2T,))
                for kt in range(NKT)])
        sB = rot.take()
        mm_acc(PB[sB][0:1, 0:TC], b_pb[sB],
               [(onecolD[:], (b_onecolD,), dxc[:, kt * TC:(kt + 1) * TC], (b_dxc,))
                for kt in range(NKT)])
        top("vector", "tensor_copy", rowS[:, 0:TC], PB[sA][0:1, 0:TC],
            reads=(b_pb[sA],), writes=(b_rowS,))
        top("vector", "tensor_tensor", rowS[:, TC:2 * TC], rowS[:, 0:TC], rowS[:, 0:TC],
            OP.mult, writes=(b_rowS,))
        top("vector", "tensor_tensor", rowS[:, 2 * TC:3 * TC], PB[sB][0:1, 0:TC],
            rowS[:, TC:2 * TC], OP.subtract, reads=(b_pb[sB],), writes=(b_rowS,))
        top("scalar", "activation", rowS[:, 3 * TC:4 * TC], rowS[:, 2 * TC:3 * TC], AF.Ln,
            bias=epsc[0:1, :], reads=(b_rowS, b_epsc), writes=(b_rowS,))
        top("scalar", "activation", rowS[:, 2 * TC:3 * TC], rowS[:, 3 * TC:4 * TC], AF.Exp,
            scale=-0.5, reads=(b_rowS,), writes=(b_rowS,))
        sC = rot.take()
        top("tensor", "matmul", PB[sC][:, 0:TC], onesrow[:], rowS[:, 0:TC],
            start=True, stop=True, reads=(b_onesrow, b_rowS), writes=(b_pb[sC],))
        sD = rot.take()
        top("tensor", "matmul", PB[sD][:, 0:TC], onesrow[:], rowS[:, 2 * TC:3 * TC],
            start=True, stop=True, reads=(b_onesrow, b_rowS), writes=(b_pb[sD],))
        fx_i = [0]
        for kt in range(NKT):
            fx_i[0] ^= 1
            fx, b_fx = (f32a, b_f32a) if fx_i[0] == 0 else (f32b, b_f32b)
            top("vector", "tensor_tensor", fx[:],
                x2T[:, kt * TC:(kt + 1) * TC], PB[sC][:, 0:TC], OP.subtract,
                reads=(b_x2T, b_pb[sC]), writes=(b_fx,))
            top("vector", "tensor_tensor", fx[:], fx[:], PB[sD][:, 0:TC], OP.mult,
                reads=(b_pb[sD],), writes=(b_fx,))
            top("vector", "tensor_scalar", fx[:], fx[:], col(C_LN2G, kt), col(C_LN2B, kt),
                OP.mult, OP.add, reads=(b_colp,), writes=(b_fx,))
            top("scalar", "copy", h2H[:, kt * TC:(kt + 1) * TC], fx[:],
                reads=(b_fx,), writes=(b_h2H,))
            top("vector", "tensor_tensor", h2L[:, kt * TC:(kt + 1) * TC], fx[:],
                h2H[:, kt * TC:(kt + 1) * TC], OP.subtract,
                reads=(b_fx, b_h2H), writes=(b_h2L,))

        fy_i = [0]
        for wave in range(2):
            for mi in range(16):
                m = wave * 16 + mi
                w_stg, b_wstg = weight_load(s_w1, m, 2 * D, "w1")
                s = rot.take()
                mm3_acc(PB[s][:, 0:TC], b_pb[s],
                        [(w_stg[:, kt * 128:(kt + 1) * 128],
                          w_stg[:, D + kt * 128: D + (kt + 1) * 128], (b_wstg,),
                          h2H[:, kt * TC:(kt + 1) * TC], h2L[:, kt * TC:(kt + 1) * TC],
                          (b_h2H, b_h2L)) for kt in range(NKT)])
                fy_i[0] ^= 1
                fx, b_fx = (f32a, b_f32a) if fy_i[0] == 0 else (f32b, b_f32b)
                top("scalar", "activation", fx[:], PB[s][:, 0:TC],
                    AF.Relu, bias=b1c[:, m:m + 1], reads=(b_pb[s], b_b1c), writes=(b_fx,))
                top("scalar", "copy", scrH[:, mi * TC:(mi + 1) * TC], fx[:],
                    reads=(b_fx,), writes=(b_scrH,))
                top("vector", "tensor_tensor", scrL[:, mi * TC:(mi + 1) * TC], fx[:],
                    scrH[:, mi * TC:(mi + 1) * TC], OP.subtract,
                    reads=(b_fx, b_scrH), writes=(b_scrL,))
            for mo in range(NKT):
                s = rot.take()
                i3 = 0
                for q in range(2):
                    hq, b_hq, hsi = next_stg()
                    offh = wave * 2048 + q * 1024
                    dma("sync", f"qw{hsi}", hq[:, 0:1024], s_w2[mo, :, offh:offh + 1024],
                        reads=(b_hq, b_scratch["w2"]), writes=(b_hq,))
                    dma("sync", f"qw{hsi}", hq[:, 1024:2048],
                        s_w2[mo, :, HID + offh:HID + offh + 1024],
                        reads=(b_hq, b_scratch["w2"]), writes=(b_hq,))
                    for j in range(8):
                        ki = q * 8 + j
                        for sap, map_, mb in (
                            (hq[:, j * 128:(j + 1) * 128],
                             scrH[:, ki * TC:(ki + 1) * TC], (b_scrH,)),
                            (hq[:, j * 128:(j + 1) * 128],
                             scrL[:, ki * TC:(ki + 1) * TC], (b_scrL,)),
                            (hq[:, 1024 + j * 128: 1024 + (j + 1) * 128],
                             scrH[:, ki * TC:(ki + 1) * TC], (b_scrH,)),
                        ):
                            def fnw2(eng, sap=sap, map_=map_, st=(i3 == 0),
                                     sp=(i3 == 47), sv=PB[s][:, 0:TC]):
                                return eng.matmul(sv, sap, map_, start=st, stop=sp)
                            ex(prog, "tensor", fnw2, reads=(b_hq,) + tuple(mb),
                               writes=(b_pb[s],))
                            i3 += 1
                if wave == 0:
                    top("vector", "scalar_tensor_tensor", sT[:, mo * TC:(mo + 1) * TC],
                        PB[s][:, 0:TC], col(C_B2, mo), x2T[:, mo * TC:(mo + 1) * TC],
                        OP.add, OP.add, reads=(b_pb[s], b_colp, b_x2T), writes=(b_sT,))
                else:
                    top("vector", "tensor_tensor", sT[:, mo * TC:(mo + 1) * TC],
                        sT[:, mo * TC:(mo + 1) * TC], PB[s][:, 0:TC], OP.add,
                        reads=(b_pb[s],), writes=(b_sT,))

        tap("sT", sT[:], b_sT)
        tap("g", shg[:], b_shg)

        for kt in range(NKT):
            top("vector", "tensor_tensor", sT[:, kt * TC:(kt + 1) * TC],
                sT[:, kt * TC:(kt + 1) * TC], shg[:, kt * TC:(kt + 1) * TC], OP.mult,
                reads=(b_shg,), writes=(b_sT,))
        for tt in range(NTT):
            dma_chained("scalar", "qr", zsb[:, tt * D:(tt + 1) * D],
                x_h[t0 + tt * 128: t0 + (tt + 1) * 128, :], reads=(b_zsb,), writes=(b_zsb,))
        for tt in range(NTT):
            for kt in range(NKT):
                s = rot.take()
                top("tensor", "transpose", PB[s][0:128, 0:128],
                    sT[:, kt * TC + tt * 128: kt * TC + (tt + 1) * 128], idt[:],
                    reads=(b_sT, b_idt), writes=(b_pb[s],))
                top("vector", "tensor_tensor", xb[:, tt * D + kt * 128: tt * D + (kt + 1) * 128],
                    zsb[:, tt * D + kt * 128: tt * D + (kt + 1) * 128], PB[s][0:128, 0:128],
                    OP.add, reads=(b_zsb, b_pb[s]), writes=(b_xb,))
            dma("scalar", "qo", out_h[t0 + tt * 128: t0 + (tt + 1) * 128, :],
                xb[:, tt * D:(tt + 1) * D], reads=(b_xb,), writes=(b_out,))

    ex(prog, "gpsimd", lambda e: e.nop(),
       extra=[(qn, prog.cnt[qn]) for qn in prog.cnt if qn not in ENGINES])

    semobjs = {}
    for name in list(prog.cnt.keys()):
        semobjs[name] = es.enter_context(nc.semaphore(f"sem_{name}"))
    with es:
        with nc.Block() as block:
            def make(engname):
                def run(eng):
                    waited = {}
                    for waits, fn, semname, amt in prog.ops[engname]:
                        for s2, v in waits.items():
                            if waited.get(s2, -1) < v:
                                eng.wait_ge(semobjs[s2], v)
                                waited[s2] = v
                        instr = fn(eng)
                        instr.then_inc(semobjs[semname], amt)
                return run
            block.sync(make("sync"))
            block.tensor(make("tensor"))
            block.vector(make("vector"))
            block.scalar(make("scalar"))
            block.gpsimd(make("gpsimd"))
    return nc


def kernel(**inputs):
    T = 4096
    x = np.ascontiguousarray(np.asarray(inputs["L0_attn_x"], dtype=np.float32))
    base = {}
    for k, v in inputs.items():
        if k == "L0_attn_x":
            continue
        base[k] = np.ascontiguousarray(np.asarray(v, dtype=np.float32))
    base["ident"] = np.eye(128, dtype=np.float32)
    sel16 = np.zeros((16, 16 * 128), dtype=np.float32)
    for n in range(16):
        sel16[n, n * 128:(n + 1) * 128] = 1.0
    base["sel16"] = sel16
    nc = build_nc(T)
    in_maps = [dict(base, x=x[c]) for c in range(8)]
    res = run_bass_kernel_spmd(nc, in_maps, core_ids=list(range(8)))
    return np.stack([np.asarray(r["out"]) for r in res.results], axis=0)


if __name__ == "__main__":
    pass


# version 11
# speedup vs baseline: 1.1538x; 1.1538x over previous
import contextlib
import numpy as np
import concourse.bass as bass
import concourse.mybir as mybir
from concourse.alu_op_type import AluOpType as OP
from concourse.bass_utils import run_bass_kernel_spmd

F32 = mybir.dt.float32
AF = mybir.ActivationFunctionType
AX = mybir.AxisListType.X

D = 1024
TC = 256
NTT = TC // 128
NKT = D // 128
DST = 16
DTR = 64
DCONV = 8
HID = 4096
HE = 512
NE = 8
EPS = 1e-5
BIG = 1e30

ENGINES = ("sync", "tensor", "vector", "scalar", "gpsimd")


class Prog:
    def __init__(self):
        self.ops = {e: [] for e in ENGINES}
        self.cnt = {}
        for e in ENGINES:
            self.cnt[e] = 0

    def bump(self, name):
        if name not in self.cnt:
            self.cnt[name] = 0

    def _mk(self, eng, fn, after, semname, amt):
        self.bump(semname)
        waits = {}
        for tok in after:
            if tok is None:
                continue
            s, v = tok
            if s == eng:
                continue
            if waits.get(s, 0) < v:
                waits[s] = v
        self.cnt[semname] += amt
        self.ops[eng].append((waits, fn, semname, amt))
        return (semname, self.cnt[semname])


class Buf:
    __slots__ = ("w", "r")

    def __init__(self):
        self.w = None
        self.r = []


def ex(prog, eng, fn, reads=(), writes=(), extra=(), q=None):
    after = list(extra)
    for b in reads:
        after.append(b.w)
    for b in writes:
        after.append(b.w)
        after.extend(b.r)
    tok = prog._mk(eng, fn, after, q if q else eng, 16 if q else 1)
    for b in reads:
        b.r.append(tok)
    for b in writes:
        b.w = tok
        b.r = []
    return tok


class Rot:

    def __init__(self):
        self.p = 0
        self.held = set()

    def take(self):
        for _ in range(12):
            s = self.p % 6
            self.p += 1
            if s not in self.held:
                return s
        raise RuntimeError("no free slot")


def build_nc(T=4096, debug=False):
    NCH = T // TC
    nc = bass.Bass("TRN2", target_bir_lowering=False)
    dram = {}

    def din(name, shape):
        dram[name] = nc.dram_tensor(name, list(shape), F32, kind="ExternalInput").ap()
        return dram[name]

    x_h = din("x", (T, D))
    w_inproj = din("L0_attn_in_proj_w", (2 * D, D))
    w_conv = din("L0_attn_conv_w", (D, DCONV))
    b_conv = din("L0_attn_conv_b", (D,))
    w_xproj = din("L0_attn_x_proj_w", (DTR + 2 * DST, D))
    w_dtproj = din("L0_attn_dt_proj_w", (D, DTR))
    b_dt = din("L0_attn_dt_proj_b", (D,))
    a_log = din("L0_attn_A_log", (D, DST))
    d_param = din("L0_attn_D_param", (D,))
    w_outproj = din("L0_attn_out_proj_w", (D, D))
    g_ln1 = din("L0_attn_ln1_g", (D,))
    b_ln1 = din("L0_attn_ln1_b", (D,))
    g_ln2 = din("L0_attn_ln2_g", (D,))
    b_ln2 = din("L0_attn_ln2_b", (D,))
    w_1 = din("L0_attn_w1", (HID, D))
    b_1 = din("L0_attn_b1", (HID,))
    w_2 = din("L0_attn_w2", (D, HID))
    b_2 = din("L0_attn_b2", (D,))
    w_rout = din("L0_ffn_routing_W", (NE, D))
    b_eb = din("L0_ffn_expert_bias", (NE,))
    w_sw1 = din("L0_ffn_sW1", (D, HE))
    b_sb1 = din("L0_ffn_sb1", (HE,))
    w_sw2 = din("L0_ffn_sW2", (HE, D))
    b_sb2 = din("L0_ffn_sb2", (D,))
    w_rw1 = din("L0_ffn_rW1", (NE, D, HE))
    b_rb1 = din("L0_ffn_rb1", (NE, HE))
    w_rw2 = din("L0_ffn_rW2", (NE, HE, D))
    b_rb2 = din("L0_ffn_rb2", (NE, D))
    ident_h = din("ident", (128, 128))
    sel_h = din("sel16", (16, 16 * 128))

    out_h = nc.dram_tensor("out", [T, D], F32, kind="ExternalOutput").ap()

    s_inproj = nc.dram_tensor("s_inproj", [16, 128, D], F32, kind="Internal").ap()
    s_outproj = nc.dram_tensor("s_outproj", [8, 128, D], F32, kind="Internal").ap()
    s_w1 = nc.dram_tensor("s_w1", [32, 128, D], F32, kind="Internal").ap()
    s_w2 = nc.dram_tensor("s_w2", [8, 128, HID], F32, kind="Internal").ap()
    s_sw1 = nc.dram_tensor("s_sw1", [4, 128, D], F32, kind="Internal").ap()
    s_sw2 = nc.dram_tensor("s_sw2", [8, 128, HE], F32, kind="Internal").ap()
    s_rw1 = nc.dram_tensor("s_rw1", [NE, 4, 128, D], F32, kind="Internal").ap()
    s_rw2 = nc.dram_tensor("s_rw2", [NE, 8, 128, HE], F32, kind="Internal").ap()

    taps = {}
    if debug:
        for nm, w in [("xnT", NKT * TC), ("x1T", NKT * TC), ("zs", NKT * TC),
                      ("xcT", NKT * TC), ("dlt", NKT * TC), ("yscan", NKT * TC),
                      ("ygate", NKT * TC), ("x2T", NKT * TC), ("h2", NKT * TC),
                      ("sT", NKT * TC), ("shared", NKT * TC), ("routed", NKT * TC),
                      ("g", NKT * TC), ("act", 16 * TC), ("dtS", TC), ("BS", TC), ("CS", TC)]:
            taps[nm] = nc.dram_tensor("tap_" + nm, [NCH, 128, w], F32,
                                      kind="ExternalOutput").ap()
    b_taps = Buf()

    prog = Prog()
    es = contextlib.ExitStack()

    def sb(name, p, f):
        t = es.enter_context(nc.sbuf_tensor(name, [p, f], F32))
        return t, Buf()

    idt, b_idt = sb("idt", 128, 128)
    sel, b_sel = sb("sel", 16, 16 * 128)
    g1bc, b_g1bc = sb("g1bc", 128, D)
    b1bc, b_b1bc = sb("b1bc", 128, D)
    onesrow, b_onesrow = sb("onesrow", 1, 128)
    onecolD, b_onecolD = sb("onecolD", 128, 1)
    colp, b_colp = sb("colp", 128, 8 * 7)
    C_LN2G, C_LN2B, C_CONVB, C_DTB, C_DP, C_B2, C_SB2 = range(7)

    def col(which, j):
        return colp[:, which * 8 + j: which * 8 + j + 1]

    sb1c, b_sb1c = sb("sb1c", 128, 4)
    b1c, b_b1c = sb("b1c", 128, 32)
    rb1c, b_rb1c = sb("rb1c", 128, 32)
    ebc, b_ebc = sb("ebc", 8, 1)
    aneg, b_aneg = sb("aneg", 128, 128)
    convw, b_convw = sb("convw", 128, 64)
    xprojT, b_xprojT = sb("xprojT", 128, 8 * 96)
    dtprojT, b_dtprojT = sb("dtprojT", 64, D)
    routT, b_routT = sb("routT", 128, 64)
    rb2t, b_rb2t = sb("rb2t", 8, D)
    stage, b_stage = sb("stage", 128, 128)
    rowstage, b_rowstage = sb("rowstage", 32, 128)
    grow, b_grow = sb("grow", 1, D)
    brow, b_brow = sb("brow", 1, D)

    xb, b_xb = sb("xb", 128, 2 * D)
    xnT, b_xnT = sb("xnT", 128, NKT * TC)
    x1y, b_x1y = sb("x1y", 128, NKT * TC)
    zsb, b_zsb = sb("zsb", 128, NKT * TC)
    xinT, b_xinT = sb("xinT", 128, NKT * (TC + 7))
    xcT, b_xcT = sb("xcT", 128, NKT * TC)
    dlt, b_dlt = sb("dlt", 128, NKT * TC)
    dxc, b_dxc = sb("dxc", 128, NKT * TC)
    x2T, b_x2T = sb("x2T", 128, NKT * TC)
    sT, b_sT = sb("sT", 128, NKT * TC)
    shg, b_shg = sb("shg", 128, NKT * TC)
    lt1, b_lt1 = sb("lt1", 128, D)
    dtS, b_dtS = sb("dtS", 64, TC)
    BS, b_BS = sb("BS", 16, TC)
    CS, b_CS = sb("CS", 16, TC)
    scr, b_scr = sb("scr", 128, 16 * TC)
    dAb = es.enter_context(nc.sbuf_tensor("dAb", [128, 2 * TC], F32))
    b_dAb = [Buf(), Buf()]
    dBxb = es.enter_context(nc.sbuf_tensor("dBxb", [128, 2 * TC], F32))
    b_dBxb = [Buf(), Buf()]
    hbb = es.enter_context(nc.sbuf_tensor("hbb", [128, 8 * TC], F32))
    b_hbb = [Buf() for _ in range(8)]
    hCb = es.enter_context(nc.sbuf_tensor("hCb", [128, 2 * TC], F32))
    b_hCb = [Buf(), Buf()]
    carry, b_carry = sb("carry", 128, 128)
    expT, b_expT = sb("expT", 128, TC)
    actS, b_actS = sb("actS", 8, TC)
    actT1, b_actT1 = sb("actT1", 8, TC)
    a_tr, b_a_tr = sb("a_tr", 128, 16)
    tmp8, b_tmp8 = sb("tmp8", 128, 16)
    cfb, b_cfb = sb("cfb", 128, 16)
    cfT, b_cfT = sb("cfT", 8, TC)
    stats, b_stats = sb("stats", 128, 32)
    rowS, b_rowS = sb("rowS", 1, 4 * TC)
    epsc, b_epsc = sb("epsc", 128, 1)
    jbuf, b_jbuf = sb("jbuf", 128, 512)
    stgs = []
    for si in range(4):
        stgs.append(sb(f"stg{si}", 128, 1024))

    PB = []
    b_pb = []
    for i in range(8):
        t = es.enter_context(nc.psum_tensor(f"pb{i}", [128, 512], F32))
        PB.append(t)
        b_pb.append(Buf())

    rot = Rot()

    b_scratch = {k: Buf() for k in ["inproj", "outproj", "w1", "w2", "sw1", "sw2", "rw1", "rw2"]}
    b_out = Buf()

    def dma(eng, q, out_ap, in_ap, reads=(), writes=(), extra=()):
        return ex(prog, eng, lambda e, o=out_ap, i=in_ap: e.dma_start(out=o, in_=i),
                  reads=reads, writes=writes, q=q, extra=extra)

    last_chain = {}

    def dma_chained(eng, q, out_ap, in_ap, reads=(), writes=()):
        extra = [last_chain.get(q)]
        tok = dma(eng, q, out_ap, in_ap, reads=reads, writes=writes, extra=extra)
        last_chain[q] = tok
        return tok

    def top(eng, fname, *args, reads=(), writes=(), **kw):
        def fn(e, fname=fname, args=args, kw=kw):
            return getattr(e, fname)(*args, **kw)
        return ex(prog, eng, fn, reads=reads, writes=writes)

    stg_i = [0]

    def next_stg():
        stg_i[0] = (stg_i[0] + 1) % 4
        t, b = stgs[stg_i[0]]
        return t, b, stg_i[0]

    def mm_acc(slot_view, slot_buf, pairs):
        n = len(pairs)
        for i, (sap, sbufs, map_, mbufs) in enumerate(pairs):
            def fn(e, sap=sap, map_=map_, st=(i == 0), sp=(i == n - 1), sv=slot_view):
                return e.matmul(sv, sap, map_, start=st, stop=sp)
            ex(prog, "tensor", fn, reads=tuple(sbufs) + tuple(mbufs), writes=(slot_buf,))

    dma_chained("sync", "qc", idt[:], ident_h[:], writes=(b_idt,))
    dma_chained("sync", "qc", sel[:], sel_h[:], writes=(b_sel,))
    top("vector", "memset", onesrow[:], 1.0, writes=(b_onesrow,))
    top("vector", "memset", onecolD[:], 1.0 / D, writes=(b_onecolD,))
    top("vector", "memset", epsc[:], EPS, writes=(b_epsc,))

    dma_chained("sync", "qc", grow[:], g_ln1.rearrange("(a d) -> a d", a=1), writes=(b_grow,))
    dma_chained("sync", "qc", brow[:], b_ln1.rearrange("(a d) -> a d", a=1), writes=(b_brow,))
    for half in range(2):
        s = rot.take()
        top("tensor", "matmul", PB[s][:, 0:512], onesrow[:], grow[:, half * 512:(half + 1) * 512],
            start=True, stop=True, reads=(b_onesrow, b_grow), writes=(b_pb[s],))
        top("vector", "tensor_copy", g1bc[:, half * 512:(half + 1) * 512], PB[s][:, 0:512],
            reads=(b_pb[s],), writes=(b_g1bc,))
        s = rot.take()
        top("tensor", "matmul", PB[s][:, 0:512], onesrow[:], brow[:, half * 512:(half + 1) * 512],
            start=True, stop=True, reads=(b_onesrow, b_brow), writes=(b_pb[s],))
        top("vector", "tensor_copy", b1bc[:, half * 512:(half + 1) * 512], PB[s][:, 0:512],
            reads=(b_pb[s],), writes=(b_b1bc,))

    def load_cols(vec_ap, which):
        dma_chained("sync", "qc", rowstage[0:8, :], vec_ap.rearrange("(k p) -> k p", p=128),
            reads=(b_rowstage,), writes=(b_rowstage,))
        s = rot.take()
        top("tensor", "transpose", PB[s][0:128, 0:8], rowstage[0:8, :], idt[0:8, 0:8],
            reads=(b_rowstage, b_idt), writes=(b_pb[s],))
        top("vector", "tensor_copy", colp[:, which * 8:(which + 1) * 8], PB[s][0:128, 0:8],
            reads=(b_pb[s],), writes=(b_colp,))

    load_cols(g_ln2, C_LN2G)
    load_cols(b_ln2, C_LN2B)
    load_cols(b_conv, C_CONVB)
    load_cols(b_dt, C_DTB)
    load_cols(d_param, C_DP)
    load_cols(b_2, C_B2)
    load_cols(b_sb2, C_SB2)

    dma_chained("sync", "qc", rowstage[0:4, :], b_sb1.rearrange("(k p) -> k p", p=128),
        reads=(b_rowstage,), writes=(b_rowstage,))
    s = rot.take()
    top("tensor", "transpose", PB[s][0:128, 0:4], rowstage[0:4, :], idt[0:4, 0:4],
        reads=(b_rowstage, b_idt), writes=(b_pb[s],))
    top("vector", "tensor_copy", sb1c[:], PB[s][0:128, 0:4], reads=(b_pb[s],), writes=(b_sb1c,))
    dma_chained("sync", "qc", rowstage[:], b_1.rearrange("(k p) -> k p", p=128),
        reads=(b_rowstage,), writes=(b_rowstage,))
    s = rot.take()
    top("tensor", "transpose", PB[s][0:128, 0:32], rowstage[:], idt[0:32, 0:32],
        reads=(b_rowstage, b_idt), writes=(b_pb[s],))
    top("vector", "tensor_copy", b1c[:], PB[s][0:128, 0:32], reads=(b_pb[s],), writes=(b_b1c,))
    dma_chained("sync", "qc", rowstage[:], b_rb1.rearrange("e (k p) -> (e k) p", p=128),
        reads=(b_rowstage,), writes=(b_rowstage,))
    s = rot.take()
    top("tensor", "transpose", PB[s][0:128, 0:32], rowstage[:], idt[0:32, 0:32],
        reads=(b_rowstage, b_idt), writes=(b_pb[s],))
    top("vector", "tensor_copy", rb1c[:], PB[s][0:128, 0:32], reads=(b_pb[s],), writes=(b_rb1c,))
    dma_chained("sync", "qc", ebc[:], b_eb.rearrange("(p a) -> p a", a=1), writes=(b_ebc,))
    for db in range(NKT):
        dma_chained("sync", "qc", stage[:, 0:16], a_log[db * 128:(db + 1) * 128, :],
            reads=(b_stage,), writes=(b_stage,))
        top("scalar", "activation", aneg[:, db * 16:(db + 1) * 16], stage[:, 0:16], AF.Exp,
            reads=(b_stage,), writes=(b_aneg,))
        top("vector", "tensor_scalar_mul", aneg[:, db * 16:(db + 1) * 16],
            aneg[:, db * 16:(db + 1) * 16], -1.0, reads=(b_aneg,), writes=(b_aneg,))
    for db in range(NKT):
        dma_chained("sync", "qc", convw[:, db * 8:(db + 1) * 8], w_conv[db * 128:(db + 1) * 128, :],
            reads=(b_convw,), writes=(b_convw,))
    for kt in range(NKT):
        dma_chained("sync", "qc", stage[0:96, :], w_xproj[:, kt * 128:(kt + 1) * 128],
            reads=(b_stage,), writes=(b_stage,))
        s = rot.take()
        top("tensor", "transpose", PB[s][0:128, 0:96], stage[0:96, :], idt[0:96, 0:96],
            reads=(b_stage, b_idt), writes=(b_pb[s],))
        top("vector", "tensor_copy", xprojT[:, kt * 96:(kt + 1) * 96], PB[s][0:128, 0:96],
            reads=(b_pb[s],), writes=(b_xprojT,))
    for m in range(NKT):
        dma_chained("sync", "qc", stage[:, 0:64], w_dtproj[m * 128:(m + 1) * 128, :],
            reads=(b_stage,), writes=(b_stage,))
        s = rot.take()
        top("tensor", "transpose", PB[s][0:64, 0:128], stage[:, 0:64], idt[:],
            reads=(b_stage, b_idt), writes=(b_pb[s],))
        top("vector", "tensor_copy", dtprojT[:, m * 128:(m + 1) * 128], PB[s][0:64, 0:128],
            reads=(b_pb[s],), writes=(b_dtprojT,))
    for kt in range(NKT):
        dma_chained("sync", "qc", stage[0:8, :], w_rout[:, kt * 128:(kt + 1) * 128],
            reads=(b_stage,), writes=(b_stage,))
        s = rot.take()
        top("tensor", "transpose", PB[s][0:128, 0:8], stage[0:8, :], idt[0:8, 0:8],
            reads=(b_stage, b_idt), writes=(b_pb[s],))
        top("vector", "tensor_copy", routT[:, kt * 8:(kt + 1) * 8], PB[s][0:128, 0:8],
            reads=(b_pb[s],), writes=(b_routT,))
    dma_chained("sync", "qc", rb2t[:], b_rb2[:], writes=(b_rb2t,))

    def phase0_transpose(w_ap, n_r, n_c, scratch_ap, sbuf_key, q="qa2"):
        bscr = b_scratch[sbuf_key]
        for r in range(n_r):
            for cg0 in range(0, n_c, 8):
                cg1 = min(cg0 + 8, n_c)
                w_stg, b_wstg, si = next_stg()
                dma("sync", f"qw{si}", w_stg[:, 0:(cg1 - cg0) * 128],
                    w_ap[r * 128:(r + 1) * 128, cg0 * 128:cg1 * 128],
                    reads=(b_wstg,), writes=(b_wstg,))
                o_stg, b_ostg, _osi = next_stg()
                for c in range(cg0, cg1):
                    s = rot.take()
                    top("tensor", "transpose", PB[s][0:128, 0:128],
                        w_stg[:, (c - cg0) * 128:(c - cg0 + 1) * 128], idt[:],
                        reads=(b_wstg, b_idt), writes=(b_pb[s],))
                    top("vector", "tensor_copy", o_stg[:, (c - cg0) * 128:(c - cg0 + 1) * 128],
                        PB[s][0:128, 0:128], reads=(b_pb[s],), writes=(b_ostg,))
                dma("gpsimd", q, scratch_ap[r, :, cg0 * 128:cg1 * 128],
                    o_stg[:, 0:(cg1 - cg0) * 128], reads=(b_ostg,), writes=(bscr,))

    def phase0_reorder(v_ap, n_kt, n_m, scratch_ap, sbuf_key, width):
        bscr = b_scratch[sbuf_key]
        v3 = v_ap.rearrange("(kt p) o -> p kt o", p=128)
        for m in range(n_m):
            w_stg, b_wstg, si = next_stg()
            dma("sync", f"qw{si}",
                w_stg[:, 0:n_kt * 128].rearrange("p (k j) -> p k j", j=128),
                v3[:, :, m * 128:(m + 1) * 128],
                reads=(b_wstg,), writes=(b_wstg,))
            dma("gpsimd", "qa2", scratch_ap[m, :, 0:width], w_stg[:, 0:width],
                reads=(b_wstg,), writes=(bscr,))

    phase0_transpose(w_inproj, 16, 8, s_inproj, "inproj", q="qa1")
    phase0_transpose(w_outproj, 8, 8, s_outproj, "outproj")
    phase0_transpose(w_1, 32, 8, s_w1, "w1")
    phase0_transpose(w_2, 8, 32, s_w2, "w2")
    phase0_reorder(w_sw1, 8, 4, s_sw1, "sw1", D)
    phase0_reorder(w_sw2, 4, 8, s_sw2, "sw2", HE)
    for e in range(NE):
        phase0_reorder(w_rw1[e], 8, 4, s_rw1[e], "rw1", D)
        phase0_reorder(w_rw2[e], 4, 8, s_rw2[e], "rw2", HE)

    for key, bscr in b_scratch.items():
        qn = "qa1" if key == "inproj" else "qa2"
        bscr.w = (qn, prog.cnt.get(qn, 0))
        bscr.r = []

    for db in range(NKT):
        top("vector", "memset", xinT[:, db * (TC + 7): db * (TC + 7) + 7], 0.0,
            writes=(b_xinT,))

    SRS, SRQ, SMEAN, SMQ, SM2, SVAR, SRSTD = range(7)

    def stc(tt, which, ln):
        j = tt * 16 + ln * 8 + which
        return stats[:, j:j + 1]

    def weight_load(scratch_ap, m, width, key):
        w_stg, b_wstg, si = next_stg()
        dma("sync", f"qw{si}", w_stg[:, 0:width], scratch_ap[m, :, 0:width],
            reads=(b_wstg, b_scratch[key]), writes=(b_wstg,))
        return w_stg, b_wstg

    for c in range(NCH):
        t0 = c * TC

        def tap(nm, src_ap, src_buf, c=c):
            if debug:
                dma("gpsimd", "qt", taps[nm][c][0:src_ap.shape[0], 0:src_ap.shape[1]],
                    src_ap, reads=(src_buf,), writes=(b_taps,))

        for tt in range(NTT):
            xv = xb[:, tt * D:(tt + 1) * D]
            dma_chained("scalar", "qx", xv, x_h[t0 + tt * 128: t0 + (tt + 1) * 128, :],
                reads=(b_xb,), writes=(b_xb,))
        for tt in range(NTT):
            xv = xb[:, tt * D:(tt + 1) * D]
            sq = dxc[:, tt * D:(tt + 1) * D]
            for ln in range(2):
                top("vector", "reduce_sum", stc(tt, SRS, ln), xv, AX,
                    reads=(b_xb,), writes=(b_stats,))
                top("scalar", "activation", sq, xv, AF.Square,
                    reads=(b_xb,), writes=(b_dxc,))
                top("vector", "reduce_sum", stc(tt, SRQ, ln), sq, AX,
                    reads=(b_dxc,), writes=(b_stats,))
                top("scalar", "mul", stc(tt, SMEAN, ln), stc(tt, SRS, ln), 1.0 / D,
                    reads=(b_stats,), writes=(b_stats,))
                top("scalar", "mul", stc(tt, SMQ, ln), stc(tt, SRQ, ln), 1.0 / D,
                    reads=(b_stats,), writes=(b_stats,))
                top("vector", "scalar_tensor_tensor", stc(tt, SVAR, ln), stc(tt, SMEAN, ln),
                    stc(tt, SMEAN, ln), stc(tt, SMQ, ln), OP.mult, OP.subtract,
                    reads=(b_stats,), writes=(b_stats,))
                top("scalar", "activation", stc(tt, SM2, ln), stc(tt, SVAR, ln), AF.Ln,
                    scale=-1.0, bias=epsc[:], reads=(b_stats, b_epsc), writes=(b_stats,))
                top("scalar", "activation", stc(tt, SRSTD, ln), stc(tt, SM2, ln), AF.Exp,
                    scale=-0.5, reads=(b_stats,), writes=(b_stats,))
                if ln == 0:
                    top("vector", "tensor_scalar", xv, xv, stc(tt, SMEAN, 0), stc(tt, SRSTD, 0),
                        OP.subtract, OP.mult, reads=(b_stats, b_xb), writes=(b_xb,))
                    for kt in range(NKT):
                        s = rot.take()
                        top("tensor", "transpose", PB[s][0:128, 0:128],
                            xv[:, kt * 128:(kt + 1) * 128], idt[:],
                            reads=(b_xb, b_idt), writes=(b_pb[s],))
                        top("vector", "tensor_copy",
                            xnT[:, kt * TC + tt * 128: kt * TC + (tt + 1) * 128],
                            PB[s][0:128, 0:128], reads=(b_pb[s],), writes=(b_xnT,))
                else:
                    top("vector", "tensor_scalar", lt1[:], xv, stc(tt, SMEAN, 1), stc(tt, SRSTD, 1),
                        OP.subtract, OP.mult, reads=(b_stats, b_xb), writes=(b_lt1,))
                    top("vector", "tensor_tensor", lt1[:], lt1[:], g1bc[:], OP.mult,
                        reads=(b_g1bc,), writes=(b_lt1,))
                    top("vector", "tensor_tensor", lt1[:], lt1[:], b1bc[:], OP.add,
                        reads=(b_b1bc,), writes=(b_lt1,))
                    for kt in range(NKT):
                        s = rot.take()
                        top("tensor", "transpose", PB[s][0:128, 0:128],
                            lt1[:, kt * 128:(kt + 1) * 128], idt[:],
                            reads=(b_lt1, b_idt), writes=(b_pb[s],))
                        top("vector", "tensor_copy",
                            x1y[:, kt * TC + tt * 128: kt * TC + (tt + 1) * 128],
                            PB[s][0:128, 0:128], reads=(b_pb[s],), writes=(b_x1y,))

        tap("xnT", xnT[:], b_xnT)
        tap("x1T", x1y[:], b_x1y)

        def conv_db(db):
            base = db * (TC + 7)
            acc = dxc[:, db * TC:(db + 1) * TC]
            top("vector", "tensor_scalar", acc, xinT[:, base: base + TC],
                convw[:, db * 8: db * 8 + 1], None, OP.mult,
                reads=(b_xinT, b_convw), writes=(b_dxc,))
            for k in range(1, DCONV):
                top("vector", "scalar_tensor_tensor", acc, xinT[:, base + k: base + k + TC],
                    convw[:, db * 8 + k: db * 8 + k + 1], acc, OP.mult, OP.add,
                    reads=(b_xinT, b_convw), writes=(b_dxc,))
            top("scalar", "activation", xcT[:, db * TC:(db + 1) * TC], acc, AF.Silu,
                bias=col(C_CONVB, db), reads=(b_dxc, b_colp), writes=(b_xcT,))

        for m in range(16):
            w_stg, b_wstg = weight_load(s_inproj, m, D, "inproj")
            s = rot.take()
            mm_acc(PB[s][:, 0:TC], b_pb[s],
                   [(w_stg[:, kt * 128:(kt + 1) * 128], (b_wstg,),
                     x1y[:, kt * TC:(kt + 1) * TC], (b_x1y,)) for kt in range(NKT)])
            if m < 8:
                base = m * (TC + 7)
                top("scalar", "copy", xinT[:, base + 7: base + 7 + TC], PB[s][:, 0:TC],
                    reads=(b_pb[s],), writes=(b_xinT,))
            else:
                top("scalar", "activation", zsb[:, (m - 8) * TC:(m - 7) * TC], PB[s][:, 0:TC],
                    AF.Silu, reads=(b_pb[s],), writes=(b_zsb,))
                conv_db(m - 8)
        tap("zs", zsb[:], b_zsb)
        tap("xcT", xcT[:], b_xcT)
        for db in range(NKT):
            base = db * (TC + 7)
            top("vector", "tensor_copy", xinT[:, base: base + 7],
                xinT[:, base + TC: base + TC + 7], reads=(b_xinT,), writes=(b_xinT,))

        s = rot.take()
        mm_acc(PB[s][0:8, 0:TC], b_pb[s],
               [(routT[:, kt * 8:(kt + 1) * 8], (b_routT,),
                 xnT[:, kt * TC:(kt + 1) * TC], (b_xnT,)) for kt in range(NKT)])
        top("scalar", "activation", actT1[:], PB[s][0:8, 0:TC], AF.Exp, bias=ebc[:],
            reads=(b_pb[s], b_ebc), writes=(b_actT1,))
        top("scalar", "activation", actT1[:], actT1[:], AF.Ln, bias=1.0,
            reads=(b_actT1,), writes=(b_actT1,))
        top("scalar", "activation", actT1[:], actT1[:], AF.Ln,
            reads=(b_actT1,), writes=(b_actT1,))
        top("scalar", "activation", actS[:], actT1[:], AF.Exp, scale=0.5,
            reads=(b_actT1,), writes=(b_actS,))
        for tt in range(NTT):
            s = rot.take()
            top("tensor", "transpose", PB[s][0:128, 0:8], actS[:, tt * 128:(tt + 1) * 128],
                idt[0:8, 0:8], reads=(b_actS, b_idt), writes=(b_pb[s],))
            top("vector", "tensor_copy", a_tr[:, tt * 8:(tt + 1) * 8], PB[s][0:128, 0:8],
                reads=(b_pb[s],), writes=(b_a_tr,))
        def vjunk():
            top("vector", "tensor_copy", jbuf[:], g1bc[:, 0:512],
                reads=(b_g1bc,), writes=(b_jbuf,))

        vjunk()
        for tt in range(NTT):
            top("vector", "reduce_max", stc(tt, 0, 0), a_tr[:, tt * 8:(tt + 1) * 8], AX,
                reads=(b_a_tr,), writes=(b_stats,))
        vjunk()
        for tt in range(NTT):
            top("vector", "tensor_scalar", tmp8[:, tt * 8:(tt + 1) * 8],
                a_tr[:, tt * 8:(tt + 1) * 8], stc(tt, 0, 0), None, OP.is_equal,
                reads=(b_a_tr, b_stats), writes=(b_tmp8,))
        vjunk()
        for tt in range(NTT):
            top("vector", "scalar_tensor_tensor", tmp8[:, tt * 8:(tt + 1) * 8],
                tmp8[:, tt * 8:(tt + 1) * 8], -BIG, a_tr[:, tt * 8:(tt + 1) * 8],
                OP.mult, OP.add, reads=(b_a_tr,), writes=(b_tmp8,))
        vjunk()
        for tt in range(NTT):
            top("vector", "reduce_max", stc(tt, 1, 0), tmp8[:, tt * 8:(tt + 1) * 8], AX,
                reads=(b_tmp8,), writes=(b_stats,))
        vjunk()
        for tt in range(NTT):
            top("vector", "tensor_scalar", cfb[:, tt * 8:(tt + 1) * 8],
                a_tr[:, tt * 8:(tt + 1) * 8], stc(tt, 1, 0), None, OP.is_ge,
                reads=(b_a_tr, b_stats), writes=(b_cfb,))
        vjunk()
        for tt in range(NTT):
            top("vector", "tensor_tensor", cfb[:, tt * 8:(tt + 1) * 8],
                cfb[:, tt * 8:(tt + 1) * 8], a_tr[:, tt * 8:(tt + 1) * 8], OP.mult,
                reads=(b_a_tr,), writes=(b_cfb,))
        vjunk()
        for tt in range(NTT):
            s = rot.take()
            top("tensor", "transpose", PB[s][0:8, 0:128], cfb[:, tt * 8:(tt + 1) * 8], idt[:],
                reads=(b_cfb, b_idt), writes=(b_pb[s],))
            top("vector", "tensor_copy", cfT[:, tt * 128:(tt + 1) * 128], PB[s][0:8, 0:128],
                reads=(b_pb[s],), writes=(b_cfT,))

        tap("act", actS[:], b_actS)
        if debug:
            dma("gpsimd", "qt", taps["act"][c][8:16, 0:TC], cfT[:],
                reads=(b_cfT,), writes=(b_taps,))

        for (p0, p1, dst, b_dst) in [(0, 64, dtS, b_dtS), (64, 80, BS, b_BS), (80, 96, CS, b_CS)]:
            s = rot.take()
            mm_acc(PB[s][0:p1 - p0, 0:TC], b_pb[s],
                   [(xprojT[:, kt * 96 + p0: kt * 96 + p1], (b_xprojT,),
                     xcT[:, kt * TC:(kt + 1) * TC], (b_xcT,)) for kt in range(NKT)])
            top("scalar", "copy", dst[:], PB[s][0:p1 - p0, 0:TC], reads=(b_pb[s],), writes=(b_dst,))
        for m in range(NKT):
            s = rot.take()
            mm_acc(PB[s][:, 0:TC], b_pb[s],
                   [(dtprojT[:, m * 128:(m + 1) * 128], (b_dtprojT,), dtS[:], (b_dtS,))])
            top("scalar", "activation", expT[:], PB[s][:, 0:TC], AF.Exp,
                bias=col(C_DTB, m), reads=(b_pb[s], b_colp), writes=(b_expT,))
            top("scalar", "activation", dlt[:, m * TC:(m + 1) * TC], expT[:], AF.Ln,
                bias=1.0, reads=(b_expT,), writes=(b_dlt,))
        for kt in range(NKT):
            top("vector", "tensor_tensor", dxc[:, kt * TC:(kt + 1) * TC],
                dlt[:, kt * TC:(kt + 1) * TC], xcT[:, kt * TC:(kt + 1) * TC], OP.mult,
                reads=(b_dlt, b_xcT), writes=(b_dxc,))

        tap("dlt", dlt[:], b_dlt)
        tap("dtS", dtS[:], b_dtS)
        tap("BS", BS[:], b_BS)
        tap("CS", CS[:], b_CS)

        def moe_steps():
            for m in range(4):
                w_stg, b_wstg = weight_load(s_sw1, m, D, "sw1")
                s = rot.take()
                mm_acc(PB[s][:, 0:TC], b_pb[s],
                       [(w_stg[:, kt * 128:(kt + 1) * 128], (b_wstg,),
                         xnT[:, kt * TC:(kt + 1) * TC], (b_xnT,)) for kt in range(NKT)])
                top("scalar", "activation", lt1[:, m * TC:(m + 1) * TC], PB[s][:, 0:TC],
                    AF.Identity, bias=sb1c[:, m:m + 1],
                    reads=(b_pb[s], b_sb1c), writes=(b_lt1,))
                yield
            for m in range(4):
                top("scalar", "activation", lt1[:, m * TC:(m + 1) * TC],
                    lt1[:, m * TC:(m + 1) * TC], AF.Silu, reads=(b_lt1,), writes=(b_lt1,))
            for m in range(NKT):
                w_stg, b_wstg = weight_load(s_sw2, m, HE, "sw2")
                s = rot.take()
                mm_acc(PB[s][:, 0:TC], b_pb[s],
                       [(w_stg[:, kt * 128:(kt + 1) * 128], (b_wstg,),
                         lt1[:, kt * TC:(kt + 1) * TC], (b_lt1,)) for kt in range(4)])
                top("scalar", "activation", shg[:, m * TC:(m + 1) * TC], PB[s][:, 0:TC],
                    AF.Identity, bias=col(C_SB2, m), reads=(b_pb[s], b_colp), writes=(b_shg,))
                yield
            for grp in range(2):
                for ei in range(4):
                    e = grp * 4 + ei
                    for mh in range(4):
                        w_stg, b_wstg = weight_load_rw1(e, mh)
                        s = rot.take()
                        mm_acc(PB[s][:, 0:TC], b_pb[s],
                               [(w_stg[:, kt * 128:(kt + 1) * 128], (b_wstg,),
                                 xnT[:, kt * TC:(kt + 1) * TC], (b_xnT,)) for kt in range(NKT)])
                        top("scalar", "activation",
                            scr[:, (ei * 4 + mh) * TC:(ei * 4 + mh + 1) * TC],
                            PB[s][:, 0:TC], AF.Identity,
                            bias=rb1c[:, e * 4 + mh: e * 4 + mh + 1],
                            reads=(b_pb[s], b_rb1c), writes=(b_scr,))
                        yield
                for j2 in range(16):
                    top("scalar", "activation", scr[:, j2 * TC:(j2 + 1) * TC],
                        scr[:, j2 * TC:(j2 + 1) * TC], AF.Silu, reads=(b_scr,), writes=(b_scr,))
                yield
                for ei in range(4):
                    e = grp * 4 + ei
                    s2 = rot.take()
                    top("tensor", "matmul", PB[s2][:, 0:TC], sel[0:8, e * 128:(e + 1) * 128],
                        cfT[:], start=True, stop=True, reads=(b_sel, b_cfT), writes=(b_pb[s2],))
                    for mh in range(4):
                        top("vector", "tensor_tensor",
                            scr[:, (ei * 4 + mh) * TC:(ei * 4 + mh + 1) * TC],
                            scr[:, (ei * 4 + mh) * TC:(ei * 4 + mh + 1) * TC], PB[s2][:, 0:TC],
                            OP.mult, reads=(b_pb[s2],), writes=(b_scr,))
                    yield
                for m in range(NKT):
                    s = rot.take()
                    for ei in range(4):
                        e = grp * 4 + ei
                        w_stg, b_wstg = weight_load_rw2(e, m)
                        for kh in range(4):
                            def fn(eng, sap=w_stg[:, kh * 128:(kh + 1) * 128],
                                   map_=scr[:, (ei * 4 + kh) * TC:(ei * 4 + kh + 1) * TC],
                                   st=(ei == 0 and kh == 0),
                                   sp=(grp == 0 and ei == 3 and kh == 3),
                                   sv=PB[s][:, 0:TC]):
                                return eng.matmul(sv, sap, map_, start=st, stop=sp)
                            ex(prog, "tensor", fn, reads=(b_wstg, b_scr), writes=(b_pb[s],))
                        yield
                    if grp == 1:
                        def fnb(eng, sap=rb2t[0:8, m * 128:(m + 1) * 128], map_=cfT[:],
                                sv=PB[s][:, 0:TC]):
                            return eng.matmul(sv, sap, map_, start=False, stop=True)
                        ex(prog, "tensor", fnb, reads=(b_rb2t, b_cfT), writes=(b_pb[s],))
                        top("vector", "tensor_tensor", xb[:, m * TC:(m + 1) * TC],
                            xb[:, m * TC:(m + 1) * TC], PB[s][:, 0:TC], OP.add,
                            reads=(b_pb[s],), writes=(b_xb,))
                    else:
                        top("vector", "tensor_copy", xb[:, m * TC:(m + 1) * TC], PB[s][:, 0:TC],
                            reads=(b_pb[s],), writes=(b_xb,))
                    yield

        def weight_load_rw1(e, mh):
            return weight_load(s_rw1[e], mh, D, "rw1")

        def weight_load_rw2(e, m):
            return weight_load(s_rw2[e], m, HE, "rw2")

        def mh_flat(e, mh):
            return 0

        moe = moe_steps()
        moe_done = [False]

        def moe_step(k=1):
            for _ in range(k):
                if moe_done[0]:
                    return
                try:
                    next(moe)
                except StopIteration:
                    moe_done[0] = True
                    return

        for n in range(DST):
            sb_slot = 6
            sc_slot = 7
            top("tensor", "matmul", PB[sb_slot][:, 0:TC], sel[:, n * 128:(n + 1) * 128], BS[:],
                start=True, stop=True, reads=(b_sel, b_BS), writes=(b_pb[sb_slot],))
            top("tensor", "matmul", PB[sc_slot][:, 0:TC], sel[:, n * 128:(n + 1) * 128], CS[:],
                start=True, stop=True, reads=(b_sel, b_CS), writes=(b_pb[sc_slot],))
            for db in range(NKT):
                ti = n * 8 + db
                i = ti % 2
                dA = dAb[:, i * TC:(i + 1) * TC]
                dBx = dBxb[:, i * TC:(i + 1) * TC]
                hb = hbb[:, db * TC:(db + 1) * TC]
                hC = hCb[:, i * TC:(i + 1) * TC]
                top("scalar", "activation", dA, dlt[:, db * TC:(db + 1) * TC], AF.Exp,
                    scale=aneg[:, db * 16 + n: db * 16 + n + 1],
                    reads=(b_dlt, b_aneg), writes=(b_dAb[i],))
                top("vector", "tensor_tensor", dBx, dxc[:, db * TC:(db + 1) * TC],
                    PB[sb_slot][:, 0:TC], OP.mult,
                    reads=(b_dxc, b_pb[sb_slot]), writes=(b_dBxb[i],))
                init = 0.0 if c == 0 else carry[:, ti:ti + 1]
                top("vector", "tensor_tensor_scan", hb, dA, dBx, init, OP.mult, OP.add,
                    reads=(b_dAb[i], b_dBxb[i], b_carry), writes=(b_hbb[db],))
                top("vector", "tensor_tensor", hC, hb, PB[sc_slot][:, 0:TC], OP.mult,
                    reads=(b_hbb[db], b_pb[sc_slot]), writes=(b_hCb[i],))
                if n == 0:
                    top("vector", "tensor_copy", x1y[:, db * TC:(db + 1) * TC], hC,
                        reads=(b_hCb[i],), writes=(b_x1y,))
                else:
                    top("vector", "tensor_tensor", x1y[:, db * TC:(db + 1) * TC],
                        x1y[:, db * TC:(db + 1) * TC], hC, OP.add,
                        reads=(b_hCb[i],), writes=(b_x1y,))
                moe_step(1)
            for db in range(NKT):
                ti = n * 8 + db
                top("vector", "tensor_copy", carry[:, ti:ti + 1],
                    hbb[:, (db + 1) * TC - 1:(db + 1) * TC],
                    reads=(b_hbb[db],), writes=(b_carry,))
            moe_step(1)
        moe_step(1000)
        tap("yscan", x1y[:], b_x1y)
        tap("shared", shg[:], b_shg)
        tap("routed", xb[:], b_xb)

        for kt in range(NKT):
            top("vector", "scalar_tensor_tensor", x1y[:, kt * TC:(kt + 1) * TC],
                xcT[:, kt * TC:(kt + 1) * TC], col(C_DP, kt), x1y[:, kt * TC:(kt + 1) * TC],
                OP.mult, OP.add, reads=(b_xcT, b_colp), writes=(b_x1y,))
        for kt in range(NKT):
            top("vector", "tensor_tensor", x1y[:, kt * TC:(kt + 1) * TC],
                x1y[:, kt * TC:(kt + 1) * TC], zsb[:, kt * TC:(kt + 1) * TC], OP.mult,
                reads=(b_zsb,), writes=(b_x1y,))

        tap("ygate", x1y[:], b_x1y)

        for kt in range(NKT):
            top("vector", "tensor_tensor", xb[:, kt * TC:(kt + 1) * TC],
                shg[:, kt * TC:(kt + 1) * TC], xb[:, kt * TC:(kt + 1) * TC], OP.add,
                reads=(b_shg,), writes=(b_xb,))
        for kt in range(NKT):
            top("scalar", "activation", shg[:, kt * TC:(kt + 1) * TC],
                xb[:, kt * TC:(kt + 1) * TC], AF.Silu, reads=(b_xb,), writes=(b_shg,))

        for m in range(NKT):
            w_stg, b_wstg = weight_load(s_outproj, m, D, "outproj")
            s = rot.take()
            mm_acc(PB[s][:, 0:TC], b_pb[s],
                   [(w_stg[:, kt * 128:(kt + 1) * 128], (b_wstg,),
                     x1y[:, kt * TC:(kt + 1) * TC], (b_x1y,)) for kt in range(NKT)])
            top("vector", "tensor_tensor", x2T[:, m * TC:(m + 1) * TC], PB[s][:, 0:TC],
                xnT[:, m * TC:(m + 1) * TC], OP.add,
                reads=(b_pb[s], b_xnT), writes=(b_x2T,))

        tap("x2T", x2T[:], b_x2T)

        for kt in range(NKT):
            top("scalar", "activation", dxc[:, kt * TC:(kt + 1) * TC],
                x2T[:, kt * TC:(kt + 1) * TC], AF.Square, reads=(b_x2T,), writes=(b_dxc,))
        sA = rot.take()
        mm_acc(PB[sA][0:1, 0:TC], b_pb[sA],
               [(onecolD[:], (b_onecolD,), x2T[:, kt * TC:(kt + 1) * TC], (b_x2T,))
                for kt in range(NKT)])
        sB = rot.take()
        mm_acc(PB[sB][0:1, 0:TC], b_pb[sB],
               [(onecolD[:], (b_onecolD,), dxc[:, kt * TC:(kt + 1) * TC], (b_dxc,))
                for kt in range(NKT)])
        top("vector", "tensor_copy", rowS[:, 0:TC], PB[sA][0:1, 0:TC],
            reads=(b_pb[sA],), writes=(b_rowS,))
        top("vector", "tensor_tensor", rowS[:, TC:2 * TC], rowS[:, 0:TC], rowS[:, 0:TC],
            OP.mult, writes=(b_rowS,))
        top("vector", "tensor_tensor", rowS[:, 2 * TC:3 * TC], PB[sB][0:1, 0:TC],
            rowS[:, TC:2 * TC], OP.subtract, reads=(b_pb[sB],), writes=(b_rowS,))
        top("scalar", "activation", rowS[:, 3 * TC:4 * TC], rowS[:, 2 * TC:3 * TC], AF.Ln,
            bias=epsc[0:1, :], reads=(b_rowS, b_epsc), writes=(b_rowS,))
        top("scalar", "activation", rowS[:, 2 * TC:3 * TC], rowS[:, 3 * TC:4 * TC], AF.Exp,
            scale=-0.5, reads=(b_rowS,), writes=(b_rowS,))
        sC = rot.take()
        top("tensor", "matmul", PB[sC][:, 0:TC], onesrow[:], rowS[:, 0:TC],
            start=True, stop=True, reads=(b_onesrow, b_rowS), writes=(b_pb[sC],))
        sD = rot.take()
        top("tensor", "matmul", PB[sD][:, 0:TC], onesrow[:], rowS[:, 2 * TC:3 * TC],
            start=True, stop=True, reads=(b_onesrow, b_rowS), writes=(b_pb[sD],))
        for kt in range(NKT):
            top("vector", "tensor_tensor", dlt[:, kt * TC:(kt + 1) * TC],
                x2T[:, kt * TC:(kt + 1) * TC], PB[sC][:, 0:TC], OP.subtract,
                reads=(b_x2T, b_pb[sC]), writes=(b_dlt,))
            top("vector", "tensor_tensor", dlt[:, kt * TC:(kt + 1) * TC],
                dlt[:, kt * TC:(kt + 1) * TC], PB[sD][:, 0:TC], OP.mult,
                reads=(b_pb[sD],), writes=(b_dlt,))
            top("vector", "tensor_scalar", dlt[:, kt * TC:(kt + 1) * TC],
                dlt[:, kt * TC:(kt + 1) * TC], col(C_LN2G, kt), col(C_LN2B, kt),
                OP.mult, OP.add, reads=(b_colp,), writes=(b_dlt,))

        tap("h2", dlt[:], b_dlt)

        for wave in range(2):
            for mi in range(16):
                m = wave * 16 + mi
                w_stg, b_wstg = weight_load(s_w1, m, D, "w1")
                s = rot.take()
                mm_acc(PB[s][:, 0:TC], b_pb[s],
                       [(w_stg[:, kt * 128:(kt + 1) * 128], (b_wstg,),
                         dlt[:, kt * TC:(kt + 1) * TC], (b_dlt,)) for kt in range(NKT)])
                top("scalar", "activation", scr[:, mi * TC:(mi + 1) * TC], PB[s][:, 0:TC],
                    AF.Relu, bias=b1c[:, m:m + 1], reads=(b_pb[s], b_b1c), writes=(b_scr,))
            for mo in range(NKT):
                s = rot.take()
                for q in range(2):
                    hq, b_hq, hsi = next_stg()
                    off = wave * 2048 + q * 1024
                    dma("sync", f"qw{hsi}", hq[:, 0:1024], s_w2[mo, :, off:off + 1024],
                        reads=(b_hq, b_scratch["w2"]), writes=(b_hq,))
                    for j in range(8):
                        ki = q * 8 + j
                        def fnw2(eng, sap=hq[:, j * 128:(j + 1) * 128],
                                 map_=scr[:, ki * TC:(ki + 1) * TC],
                                 st=(ki == 0), sp=(ki == 15), sv=PB[s][:, 0:TC]):
                            return eng.matmul(sv, sap, map_, start=st, stop=sp)
                        ex(prog, "tensor", fnw2, reads=(b_hq, b_scr), writes=(b_pb[s],))
                if wave == 0:
                    top("vector", "scalar_tensor_tensor", sT[:, mo * TC:(mo + 1) * TC],
                        PB[s][:, 0:TC], col(C_B2, mo), x2T[:, mo * TC:(mo + 1) * TC],
                        OP.add, OP.add, reads=(b_pb[s], b_colp, b_x2T), writes=(b_sT,))
                else:
                    top("vector", "tensor_tensor", sT[:, mo * TC:(mo + 1) * TC],
                        sT[:, mo * TC:(mo + 1) * TC], PB[s][:, 0:TC], OP.add,
                        reads=(b_pb[s],), writes=(b_sT,))

        tap("sT", sT[:], b_sT)
        tap("g", shg[:], b_shg)

        for kt in range(NKT):
            top("vector", "tensor_tensor", sT[:, kt * TC:(kt + 1) * TC],
                sT[:, kt * TC:(kt + 1) * TC], shg[:, kt * TC:(kt + 1) * TC], OP.mult,
                reads=(b_shg,), writes=(b_sT,))
        for tt in range(NTT):
            dma_chained("scalar", "qr", zsb[:, tt * D:(tt + 1) * D],
                x_h[t0 + tt * 128: t0 + (tt + 1) * 128, :], reads=(b_zsb,), writes=(b_zsb,))
        for tt in range(NTT):
            for kt in range(NKT):
                s = rot.take()
                top("tensor", "transpose", PB[s][0:128, 0:128],
                    sT[:, kt * TC + tt * 128: kt * TC + (tt + 1) * 128], idt[:],
                    reads=(b_sT, b_idt), writes=(b_pb[s],))
                top("vector", "tensor_tensor", xb[:, tt * D + kt * 128: tt * D + (kt + 1) * 128],
                    zsb[:, tt * D + kt * 128: tt * D + (kt + 1) * 128], PB[s][0:128, 0:128],
                    OP.add, reads=(b_zsb, b_pb[s]), writes=(b_xb,))
            dma("scalar", "qo", out_h[t0 + tt * 128: t0 + (tt + 1) * 128, :],
                xb[:, tt * D:(tt + 1) * D], reads=(b_xb,), writes=(b_out,))

    ex(prog, "gpsimd", lambda e: e.nop(),
       extra=[(qn, prog.cnt[qn]) for qn in prog.cnt if qn not in ENGINES])

    semobjs = {}
    for name in list(prog.cnt.keys()):
        semobjs[name] = es.enter_context(nc.semaphore(f"sem_{name}"))
    with es:
        with nc.Block() as block:
            def make(engname):
                def run(eng):
                    waited = {}
                    for waits, fn, semname, amt in prog.ops[engname]:
                        for s2, v in waits.items():
                            if waited.get(s2, -1) < v:
                                eng.wait_ge(semobjs[s2], v)
                                waited[s2] = v
                        instr = fn(eng)
                        instr.then_inc(semobjs[semname], amt)
                return run
            block.sync(make("sync"))
            block.tensor(make("tensor"))
            block.vector(make("vector"))
            block.scalar(make("scalar"))
            block.gpsimd(make("gpsimd"))
    return nc


def kernel(**inputs):
    T = 4096
    x = np.ascontiguousarray(np.asarray(inputs["L0_attn_x"], dtype=np.float32))
    base = {}
    for k, v in inputs.items():
        if k == "L0_attn_x":
            continue
        base[k] = np.ascontiguousarray(np.asarray(v, dtype=np.float32))
    base["ident"] = np.eye(128, dtype=np.float32)
    sel16 = np.zeros((16, 16 * 128), dtype=np.float32)
    for n in range(16):
        sel16[n, n * 128:(n + 1) * 128] = 1.0
    base["sel16"] = sel16
    nc = build_nc(T)
    in_maps = [dict(base, x=x[c]) for c in range(8)]
    res = run_bass_kernel_spmd(nc, in_maps, core_ids=list(range(8)))
    return np.stack([np.asarray(r["out"]) for r in res.results], axis=0)


if __name__ == "__main__":
    pass
